# version 18
# baseline (speedup 1.0000x reference)
import contextlib
import numpy as np
import ml_dtypes
import concourse.bass as bass
import concourse.mybir as mybir
from concourse.bass_utils import run_bass_kernel_spmd

F32 = mybir.dt.float32
BF16 = mybir.dt.bfloat16
AF = mybir.ActivationFunctionType
OP = mybir.AluOpType
BF = ml_dtypes.bfloat16

NCORE = 8
C = 1024
CH = 8
T = 2048
NT = 4
L = 4
FFN = 4096
FCH = 32
DELTA = 4.0
EPS = 1e-5
LN2 = float(np.log(np.float32(2.0)))
R5 = float(np.sqrt(np.float32(0.5)))
AX = mybir.AxisListType.X

ENG = ("t", "v", "s", "g", "y")


class Sched:
    def __init__(self, semnames):
        self.stream = {e: [] for e in ENG}
        self.cnt = {n: 0 for n in semnames}
        self.lastwait = {e: {n: 0 for n in semnames} for e in ENG}

    def op(self, eng, fn, after=(), inc=None, amt=1, par=False):
        waits = {}
        for tk in after:
            if tk is None:
                continue
            s, v = tk
            if v > self.lastwait[eng].get(s, 0):
                waits[s] = max(waits.get(s, 0), v)
        for s, v in waits.items():
            self.lastwait[eng][s] = v
        tok = None
        if inc is not None:
            self.cnt[inc] += amt
            tok = (inc, self.cnt[inc])
            self.lastwait[eng][inc] = self.cnt[inc]
        self.stream[eng].append((fn, sorted(waits.items()), tok, amt, par))
        return tok


def build_program():
    nc = bass.Bass("TRN2", target_bir_lowering=False, detect_race_conditions=False)
    dram = {}

    def din(name, shape, dt=F32):
        dram[name] = nc.dram_tensor(name, shape, dt, kind="ExternalInput").ap()
        return dram[name]

    x_in = din("x", [C, T])
    for l in range(L):
        din(f"wk{l}", [4, 128, 2048]); din(f"wv{l}", [4, 128, 2048])
        din(f"wr{l}", [4, 128, 2048]); din(f"wo{l}", [4, 128, 2048])
        din(f"wck{l}", [16, 128, 2048]); din(f"wcv{l}", [16, 128, 2048])
        din(f"wcr{l}", [4, 128, 2048])
        for p in ("ln1w", "mxk", "mxv", "mxr", "td", "tf", "ln2w", "cmk", "cmr"):
            din(f"{p}{l}", [128, CH])
    din("w8", [4, 128, 2048])
    din("lnfw", [128, CH]); din("fbias", [128, CH])
    din("scol", [128, 1]); din("wscol", [128, 1])
    din("statlh", [128, 16], BF16)
    din("eyeb", [4, 512])
    din("trow", [128, T])
    din("maskh", [128, 64])
    din("mtot_steps", [128, 128])
    din("mtot_mask", [128, 128])

    out_d = nc.dram_tensor("o", [C, T], F32, kind="ExternalOutput").ap()
    wckb_d = [nc.dram_tensor(f"wckb{l}", [16, 128, 2048], BF16).ap() for l in range(L)]
    wcvb_d = [nc.dram_tensor(f"wcvb{l}", [16, 128, 2048], BF16).ap() for l in range(L)]
    acc_d = nc.dram_tensor("acc_d", [C, T], F32).ap()
    res_d = nc.dram_tensor("res_d", [C, T], F32).ap()
    ccb = {}
    for l in range(L):
        for nm, w in (("hg", 8), ("h2", 8), ("tt", 16)):
            ccb[f"{nm}{l}_i"] = nc.dram_tensor(f"{nm}{l}_i", [128, w], F32).ap()
            ccb[f"{nm}{l}_o"] = nc.dram_tensor(f"{nm}{l}_o", [128 * NCORE, w], F32).ap()

    sbt = lambda n, s, dt=F32: nc.alloc_sbuf_tensor(n, s, dt).ap()
    AR = [sbt(f"ar{i}", [128, 16384], BF16) for i in range(5)]
    HALF = sbt("half", [128, 8192], BF16)
    WB = [sbt(f"wb{i}", [128, 2048], BF16) for i in range(2)]
    TROW = sbt("trow_s", [128, T])
    STATLH = sbt("statlh_s", [128, 16], BF16)
    EYEB = sbt("eyeb_s", [4, 512])
    MASKH = sbt("maskh_s", [128, 64])
    MTS = sbt("mts_s", [128, 128])
    MTM = sbt("mtm_s", [128, 128])
    ST1 = sbt("st1", [4, 512]); ST2F = sbt("st2", [128, 512]); STS = sbt("sts", [4, 512])
    ST2 = ST2F[0:4, :]
    CP = sbt("cpool", [128, 16])
    pnames = ("ln1w", "mxk", "mxv", "mxr", "td", "tf", "ln2w", "cmk", "cmr",
              "ak", "bk", "av", "bv", "arr", "brr", "ack", "bck", "acr", "bcr",
              "wdec", "eu", "lnfw", "fbias", "nw", "b2k")
    P = {n: sbt("p_" + n, [128, CH]) for n in pnames}
    SCOL = sbt("p_scol", [128, 1]); WSCOL = sbt("p_wscol", [128, 1])
    SINV = sbt("p_sinv", [128, 1]); SWS = sbt("p_sws", [128, 1])
    W16 = sbt("w16", [128, 16])
    WREP = sbt("wrep", [128, 128])
    COEF = sbt("coef", [128, 128])
    TOTS = sbt("tots", [128, 16])
    ABINIT = sbt("abinit", [128, 16])
    HSEND = sbt("hsend", [128, 8])
    HRECV = sbt("hrecv", [128, 64])
    TRECV = sbt("trecv", [128, 128])
    PREVC = sbt("prevc", [128, 8])
    JUNK = sbt("junk", [128, 512])

    PB = [nc.alloc_psum_tensor(f"pb{i}", [128, 512], F32).ap() for i in range(8)]
    ASH = AR[0].bitcast(F32)[:, 0:T + 8]
    BSH = AR[0].bitcast(F32)[:, T + 8:2 * T + 16]
    EWB = AR[0].bitcast(F32)[:, 2 * T + 16:3 * T + 16]

    def ar_f32(i):
        return AR[i].bitcast(F32)

    def res_c(c):
        a = ar_f32(c // 4)
        return a[:, (c % 4) * T:(c % 4) * T + T]

    def abf(i, c):
        return AR[i][:, c * T:(c + 1) * T]

    def hbf(j):
        return HALF[:, j * T:(j + 1) * T]

    def hf32(j):
        return HALF.bitcast(F32)[:, j * T:(j + 1) * T]

    sched = Sched(["T", "V", "S", "G", "GD", "Y", "CC"])
    o = sched.op

    CONSTS = {0: 0.0, 1: 1.0, 2: -1.0, 3: -DELTA, 4: EPS, 5: LN2,
              6: -LN2, 7: -9.0 * LN2, 8: 9.0 * LN2, 9: -0.5}
    def emit_cpool(v):
        last = None
        for j, val in CONSTS.items():
            last = v.memset(CP[:, j:j + 1], val)
        return last
    tok_cp = o("v", emit_cpool, inc="V")
    for j, val in CONSTS.items():
        nc.const_aps.register(F32, val, CP[:, j:j + 1])
    def cpc(val, p=128):
        for j, v in CONSTS.items():
            if v == val:
                return CP[0:p, j:j + 1]
        raise KeyError(val)

    def dma(eng, dst, src, after=()):
        sem = "Y" if eng == "y" else "GD"
        def f(e, dst=dst, src=src):
            return e.dma_start(out=dst, in_=src)
        return o(eng, f, after=after, inc=sem, amt=16)

    for nm, dst in (("trow", TROW), ("statlh", STATLH), ("eyeb", EYEB),
                    ("maskh", MASKH), ("mtot_steps", MTS), ("mtot_mask", MTM),
                    ("scol", SCOL), ("wscol", WSCOL)):
        dma("y", dst, dram[nm])
    dma("y", P["lnfw"], dram["lnfw"])
    dma("y", P["fbias"], dram["fbias"])
    tok_x = None
    for c in range(CH):
        tok_x = dma("y", res_c(c), x_in[c * 128:(c + 1) * 128, :])
    TOK_LOADS = tok_x

    tok_scal = o("v", lambda v: (v.reciprocal(out=SINV, in_=SCOL),
                                 v.tensor_tensor(out=SWS, in0=SCOL, in1=WSCOL, op=OP.mult))[-1],
                 after=[TOK_LOADS, tok_cp], inc="V")


    wb_turn = [0]
    wb_busy = [None, None]

    def load_w_piece(src_ap, cast, after=()):
        i = wb_turn[0]; wb_turn[0] ^= 1
        extra = [wb_busy[i]] if wb_busy[i] is not None else []
        eng = "g" if cast else "y"
        t_ = dma(eng, WB[i], src_ap, after=list(after) + extra)
        return i, t_

    def wsl(i, kc, off, width=128):
        return WB[i][:, kc * 256 + off:kc * 256 + off + width]

    def emit_stats(src_fn, after_toks):
        t_xb = o("s", (lambda s: [s.copy(out=abf(2, c), in_=src_fn(c))
                                  for c in range(CH)][-1]),
                 after=list(after_toks), inc="S", par=True)
        t_prev_mm = None
        mm_hist = {}
        for c in range(CH):
            slot = c % 4
            t_sq = o("s", (lambda s, c=c, slot=slot:
                           s.activation(out=hbf(slot), in_=abf(2, c), func=AF.Square)),
                     after=[t_xb] + ([mm_hist[c - 4]] if (c - 4) in mm_hist else []), inc="S")
            def f_mm(t, c=c, slot=slot):
                last = None
                for nt in range(NT):
                    last = t.matmul(PB[4][0:4, :], STATLH[:, 4 * nt:4 * nt + 4],
                                    abf(2, c)[:, nt * 512:(nt + 1) * 512],
                                    start=(c == 0 and nt == 0), stop=(c == CH - 1 and nt == NT - 1))
                for nt in range(NT):
                    last = t.matmul(PB[5][0:4, :], STATLH[:, 4 * nt:4 * nt + 4],
                                    hbf(slot)[:, nt * 512:(nt + 1) * 512],
                                    start=(c == 0 and nt == 0), stop=(c == CH - 1 and nt == NT - 1))
                return last
            t_prev_mm = o("t", f_mm, after=[t_sq], inc="T")
            mm_hist[c] = t_prev_mm
        def f_stat(v):
            v.tensor_scalar(out=ST1, in0=PB[4][0:4, :], scalar1=1.0 / C, scalar2=None, op0=OP.mult)
            v.tensor_scalar(out=ST2, in0=PB[5][0:4, :], scalar1=1.0 / C, scalar2=None, op0=OP.mult)
            v.tensor_tensor(out=STS, in0=ST1, in1=ST1, op=OP.mult)
            v.tensor_tensor(out=ST2, in0=ST2, in1=STS, op=OP.subtract)
            return v.tensor_scalar(out=ST2, in0=ST2, scalar1=EPS, scalar2=None, op0=OP.add)
        t_v = o("v", f_stat, after=[t_prev_mm], inc="V")
        t_rs = o("s", (lambda s: (s.activation(out=STS, in_=ST2, func=AF.Ln),
                                  s.activation(out=STS, in_=STS, func=AF.Exp, scale=cpc(-0.5, 4)))[-1]),
                 after=[t_v], inc="S")
        t_mu = o("v", (lambda v: v.tensor_tensor(out=ST1, in0=ST1, in1=STS, op=OP.mult)),
                 after=[t_rs], inc="V")
        return t_mu

    def emit_bcast(nt, after_toks, pair=(6, 7)):
        def f(t, nt=nt, pair=pair):
            t.matmul(PB[pair[0]], EYEB[:, 128 * nt:128 * (nt + 1)], STS, start=True, stop=True)
            return t.matmul(PB[pair[1]], EYEB[:, 128 * nt:128 * (nt + 1)], ST1, start=True, stop=True)
        return o("t", f, after=after_toks, inc="T")

    def emit_norm_apply(src_fn, emitfn, t_mu):
        t_prev = None
        for nt in range(NT):
            t_b = emit_bcast(nt, [t_mu] + ([t_prev] if t_prev else []))
            for c in range(CH):
                sl = slice(nt * 512, (nt + 1) * 512)
                def f(v, c=c, sl=sl):
                    return emitfn(v, c, sl)
                t_prev = o("v", f, after=[t_b], inc="V")
        return t_prev

    def emit_gfast(t_mu):
        JB = JUNK.bitcast(BF16)
        JB2 = ST2F.bitcast(BF16)
        t_prev = None
        t_f2 = {}
        for nt in range(NT):
            pair = (6, 7) if nt % 2 == 0 else (4, 5)
            jb = JB if nt % 2 == 0 else JB2
            t_b = emit_bcast(nt, [t_mu] + ([t_f2[nt - 2]] if (nt - 2) in t_f2 else []), pair=pair)
            t_e = o("s", (lambda s, jb=jb, pair=pair: (s.copy(out=jb[:, 0:512], in_=PB[pair[0]]),
                                                      s.copy(out=jb[:, 512:1024], in_=PB[pair[1]]))[-1]),
                    after=[t_b], inc="S")
            def f1(v, nt=nt, jb=jb):
                last = None
                for c in range(CH):
                    sl = slice(nt * 512, (nt + 1) * 512)
                    last = v.tensor_tensor(out=abf(2, c)[:, sl], in0=abf(2, c)[:, sl],
                                           in1=jb[:, 0:512], op=OP.mult)
                return last
            t_1 = o("v", f1, after=[t_e], inc="V", par=True)
            def f2(v, nt=nt, jb=jb):
                last = None
                for c in range(CH):
                    sl = slice(nt * 512, (nt + 1) * 512)
                    last = v.tensor_tensor(out=abf(2, c)[:, sl], in0=abf(2, c)[:, sl],
                                           in1=jb[:, 512:1024], op=OP.subtract)
                return last
            t_prev = o("v", f2, after=[t_1], inc="V", par=True)
            t_f2[nt] = t_prev
        return t_prev

    t_mu = emit_stats(res_c, [TOK_LOADS, tok_cp])
    def f_xn(v, c, sl):
        v.tensor_tensor(out=JUNK, in0=res_c(c)[:, sl], in1=PB[6], op=OP.mult)
        return v.tensor_tensor(out=res_c(c)[:, sl], in0=JUNK, in1=PB[7], op=OP.subtract)
    TOK_XN = emit_norm_apply(res_c, f_xn, t_mu)

    xn_d = nc.dram_tensor("xn_d", [C, T], F32).ap()
    ydq_d = nc.dram_tensor("ydq_d", [C, T], BF16).ap()
    t_xp = None
    for c in range(CH):
        t_xp = dma("y", xn_d[c * 128:(c + 1) * 128, :], res_c(c), after=[TOK_XN])
    TOK_XNPARK = t_xp

    def qslot(j):
        return ar_f32(j // 4)[:, (j % 4) * T:(j % 4) * T + T]

    ydq_toks = []

    def emit_quant_chunk(c, after_extra):
        y, ay, z, e_, q, qi, r, m = (qslot(j) for j in range(8))
        stage = AR[0][:, 0:T]
        t_x = dma("y", y, xn_d[c * 128:(c + 1) * 128, :], after=list(after_extra))
        def f1(v, y=y, ay=ay):
            v.tensor_scalar(out=y, in0=y, scalar1=SINV, scalar2=448.0, op0=OP.mult, op1=OP.min)
            v.tensor_scalar(out=y, in0=y, scalar1=-448.0, scalar2=None, op0=OP.max)
            v.tensor_scalar(out=ay, in0=y, scalar1=-1.0, scalar2=None, op0=OP.mult)
            return v.tensor_tensor(out=ay, in0=ay, in1=y, op=OP.max)
        t1 = o("v", f1, after=[t_x, tok_scal] + list(after_extra), inc="V")
        t2 = o("s", (lambda s, ay=ay, z=z: (s.activation(out=z, in_=ay, func=AF.Square),
                                            s.activation(out=z, in_=z, func=AF.Ln))[-1]),
               after=[t1], inc="S")
        def f3(v, z=z, e_=e_):
            v.tensor_scalar(out=z, in0=z, scalar1=1.0 / (2 * LN2), scalar2=-6.0, op0=OP.mult, op1=OP.max)
            return v.tensor_scalar(out=e_, in0=z, scalar1=-5.0, scalar2=1.0, op0=OP.is_ge, op1=OP.mult)
        t_e = o("v", f3, after=[t2], inc="V")
        for k in range(-4, 9):
            def fe(en, z=z, e_=e_, k=k):
                return en.scalar_tensor_tensor(out=e_, in0=z, scalar=float(k), in1=e_,
                                               op0=OP.is_ge, op1=OP.add)
            t_e = o("v", fe, after=[t_e], inc="V")
        t4 = o("s", (lambda s, e_=e_, q=q, qi=qi:
                     (s.activation(out=q, in_=e_, func=AF.Exp, scale=cpc(LN2), bias=cpc(-9.0 * LN2)),
                      s.activation(out=qi, in_=e_, func=AF.Exp, scale=cpc(-LN2), bias=cpc(9.0 * LN2)))[-1]),
               after=[t_e], inc="S")
        def f5(v, ay=ay, qi=qi, r=r, m=m):
            v.tensor_tensor(out=r, in0=ay, in1=qi, op=OP.mult)
            return v.tensor_scalar(out=m, in0=r, scalar1=0.5, scalar2=1.0, op0=OP.is_gt, op1=OP.mult)
        t_m = o("v", f5, after=[t4], inc="V")
        for jj in range(1, 16):
            def fm(en, r=r, m=m, jj=jj):
                return en.scalar_tensor_tensor(out=m, in0=r, scalar=jj + 0.5, in1=m,
                                               op0=OP.is_gt, op1=OP.add)
            t_m = o("v", fm, after=[t_m], inc="V")
        t_sg = o("s", (lambda s, y=y, ay=ay: s.sign(out=ay, in_=y)), after=[t_m], inc="S")
        def f6(v, m=m, q=q, ay=ay, stage=stage):
            v.tensor_tensor(out=m, in0=m, in1=q, op=OP.mult)
            return v.tensor_tensor(out=stage, in0=m, in1=ay, op=OP.mult)
        t6 = o("v", f6, after=[t_sg], inc="V")
        t_out = dma("y", ydq_d[c * 128:(c + 1) * 128, :], stage, after=[t6])
        ydq_toks.append(t_out)
        return t6, t_out

    FFN_STATE = {}

    def emit_ffn(after_banks, after_ar2):
        t_yl = None
        for kc in range(CH):
            t_yl = dma("y", AR[2][:, kc * T:(kc + 1) * T], ydq_d[kc * 128:(kc + 1) * 128, :],
                       after=[after_ar2] + list(ydq_toks[-1:]))
        pieces = [load_w_piece(dram["w8"][0], cast=True),
                  load_w_piece(dram["w8"][1], cast=True)]
        ev = {0: None, 1: None}
        t_prev_mm = None
        t_st = None
        t_ac_last = None
        fx = ar_f32(4)[:, 0:512]
        fo = ar_f32(4)[:, 512:1024]
        for mc in range(CH):
            grp = mc % 2
            banks = PB[4 * grp:4 * grp + 4]
            while len(pieces) <= mc // 2:
                h = len(pieces)
                pieces.append(load_w_piece(dram["w8"][h], cast=True))
            bi, t_wp = pieces[mc // 2]
            aft = [t_yl, t_wp] + list(after_banks)
            if t_prev_mm is not None:
                aft.append(t_prev_mm)
            if ev[grp] is not None:
                aft.append(ev[grp])
            def f_mm(t, mc=mc, banks=banks, bi=bi):
                last = None
                for kc in range(CH):
                    w_ = wsl(bi, kc, (mc % 2) * 128)
                    for nt in range(NT):
                        last = t.matmul(banks[nt], w_, AR[2][:, kc * T + nt * 512:kc * T + (nt + 1) * 512],
                                        start=(kc == 0), stop=(kc == CH - 1))
                return last
            t_mm = o("t", f_mm, after=aft, inc="T")
            wb_busy[bi] = t_mm
            last_ev = None
            for nt in range(NT):
                sl = slice(nt * 512, (nt + 1) * 512)
                t_xo = dma("y", fx, x_in[mc * 128:(mc + 1) * 128, sl],
                           after=[t_mm] + ([t_ac_last] if t_ac_last else []))
                def f_ev(v, nt=nt, mc=mc, banks=banks, fo=fo):
                    return v.tensor_scalar(out=fo, in0=banks[nt], scalar1=SWS,
                                           scalar2=P["fbias"][:, mc:mc + 1],
                                           op0=OP.mult, op1=OP.add)
                t_ev = o("v", f_ev, after=[t_mm, tok_scal] + ([t_st] if t_st else []), inc="V")
                def f_acc(v, fo=fo, fx=fx):
                    return v.tensor_tensor(out=fo, in0=fo, in1=fx, op=OP.add)
                t_ac = o("v", f_acc, after=[t_ev, t_xo], inc="V")
                t_st = dma("y", acc_d[mc * 128:(mc + 1) * 128, sl], fo, after=[t_ac])
                t_ac_last = t_ac
                last_ev = t_ev
            ev[grp] = last_ev
            t_prev_mm = t_mm
        FFN_STATE["acc"] = t_st
        FFN_STATE["banks"] = max(ev[0], ev[1], key=lambda x: x[1])
        FFN_STATE["mm"] = t_prev_mm

    def emit_params(l, after_toks):
        t_ld = None
        for p in ("ln1w", "mxk", "mxv", "mxr", "td", "tf", "ln2w", "cmk", "cmr"):
            t_ld = dma("y", P[p], dram[f"{p}{l}"], after=after_toks)
        def f_v(v):
            for (a, b, mx, w_, sc) in (("ak", "bk", "mxk", "ln1w", None),
                                       ("av", "bv", "mxv", "ln1w", 0.5),
                                       ("arr", "brr", "mxr", "ln1w", 0.5),
                                       ("ack", "bck", "cmk", "ln2w", R5),
                                       ("acr", "bcr", "cmr", "ln2w", 0.5)):
                v.tensor_tensor(out=P[a], in0=P[w_], in1=P[mx], op=OP.mult)
                v.tensor_tensor(out=P[b], in0=P[w_], in1=P[a], op=OP.subtract)
                if sc is not None:
                    v.tensor_scalar(out=P[a], in0=P[a], scalar1=sc, scalar2=None, op0=OP.mult)
                    v.tensor_scalar(out=P[b], in0=P[b], scalar1=sc, scalar2=None, op0=OP.mult)
            return v.nop()
        t_pv = o("v", f_v, after=[t_ld, tok_cp], inc="V")
        t_s = o("s", (lambda s: (s.activation(out=P["wdec"], in_=P["td"], func=AF.Exp),
                                 s.activation(out=P["eu"], in_=P["tf"], func=AF.Exp))[-1]),
                after=[t_ld], inc="S")
        def f_w(v):
            v.tensor_scalar(out=P["wdec"], in0=P["wdec"], scalar1=-1.0, scalar2=None, op0=OP.mult)
            v.tensor_scalar(out=P["nw"], in0=P["wdec"], scalar1=-1.0, scalar2=None, op0=OP.mult)
            return v.tensor_scalar(out=P["b2k"], in0=P["wdec"], scalar1=2047.0, scalar2=None, op0=OP.mult)
        t_pw = o("v", f_w, after=[t_s], inc="V")
        return t_pv, t_pw

    hrecv_free = [None]
    def emit_halo_ag(l, kind, src_tok):
        i_b = ccb[f"{kind}{l}_i"]; o_b = ccb[f"{kind}{l}_o"]
        t_out = dma("y", i_b, HSEND, after=[src_tok])
        def f_cc(g, i_b=i_b, o_b=o_b):
            return g.collective_compute("AllGather", OP.bypass,
                                        replica_groups=[list(range(NCORE))],
                                        ins=[i_b.opt()], outs=[o_b.opt()])
        t_cc = o("g", f_cc, after=[t_out], inc="CC")
        t_in = dma("y", HRECV.rearrange("p (r c) -> p r c", r=8),
                   o_b.rearrange("(r p) c -> p r c", p=128),
                   after=[t_cc] + ([hrecv_free[0]] if hrecv_free[0] else []))
        def f_cmb(v):
            v.tensor_tensor(out=HRECV, in0=HRECV, in1=MASKH, op=OP.mult)
            return v.reduce_sum(out=PREVC, in_=HRECV.rearrange("p (r c) -> p c r", r=8),
                                axis=AX)
        t_c = o("v", f_cmb, after=[t_in], inc="V")
        hrecv_free[0] = t_c
        return t_c, t_out

    def emit_mixed(dst_i, acol, bcol, after_toks, after_col0=()):
        def f1(v):
            last = None
            for c in range(CH):
                last = v.tensor_scalar(out=abf(dst_i, c)[:, 1:T], in0=AR[2][:, c * T:c * T + T - 1],
                                       scalar1=bcol[:, c:c + 1], scalar2=None, op0=OP.mult)
            return last
        t1 = o("v", f1, after=after_toks, inc="V", par=True)
        def f2(v):
            last = None
            for c in range(CH):
                last = v.scalar_tensor_tensor(out=abf(dst_i, c)[:, 1:T], in0=AR[2][:, c * T + 1:c * T + T],
                                              scalar=acol[:, c:c + 1], in1=abf(dst_i, c)[:, 1:T],
                                              op0=OP.mult, op1=OP.add)
            return last
        t2 = o("v", f2, after=[t1], inc="V", par=True)
        def f3(v):
            last = None
            for c in range(CH):
                last = v.tensor_scalar(out=abf(dst_i, c)[:, 0:1], in0=PREVC[:, c:c + 1],
                                       scalar1=bcol[:, c:c + 1], scalar2=None, op0=OP.mult)
            return last
        t3 = o("v", f3, after=[t2] + list(after_col0), inc="V", par=True)
        def f4(v):
            last = None
            for c in range(CH):
                last = v.scalar_tensor_tensor(out=abf(dst_i, c)[:, 0:1], in0=AR[2][:, c * T:c * T + 1],
                                              scalar=acol[:, c:c + 1], in1=abf(dst_i, c)[:, 0:1],
                                              op0=OP.mult, op1=OP.add)
            return last
        return o("v", f4, after=[t3], inc="V", par=True)

    def emit_cc_proj(wn, rhs_i, evac_func, after_rhs, t_guard, evac_extra=()):
        pieces = [load_w_piece(dram[wn][0], cast=True),
                  load_w_piece(dram[wn][1], cast=True)]
        ev = {0: None, 1: None}
        ev_mc = {}
        t_prev_mm = t_guard
        for mc in range(CH):
            grp = mc % 2
            banks = PB[4 * grp:4 * grp + 4]
            if mc % 2 == 0 and mc // 2 + 1 < 4 and len(pieces) < mc // 2 + 2:
                pass
            while len(pieces) <= mc // 2:
                h = len(pieces)
                pieces.append(load_w_piece(dram[wn][h], cast=True))
            bi, t_wp = pieces[mc // 2]
            aft = [after_rhs, t_wp]
            if t_prev_mm is not None:
                aft.append(t_prev_mm)
            if ev[grp] is not None:
                aft.append(ev[grp])
            def f_mm(t, mc=mc, banks=banks, bi=bi):
                last = None
                for kc in range(CH):
                    w_ = wsl(bi, kc, (mc % 2) * 128)
                    for nt in range(NT):
                        last = t.matmul(banks[nt], w_, abf(rhs_i, kc)[:, nt * 512:(nt + 1) * 512],
                                        start=(kc == 0), stop=(kc == CH - 1))
                return last
            t_mm = o("t", f_mm, after=aft, inc="T")
            wb_busy[bi] = t_mm
            def f_ev(s, mc=mc, banks=banks):
                last = None
                for nt in range(NT):
                    last = evac_func(s, mc, nt, banks[nt])
                return last
            ev[grp] = o("s", f_ev, after=[t_mm] + list(evac_extra), inc="S")
            ev_mc[mc] = ev[grp]
            t_prev_mm = t_mm
        t_ev = max(ev[0], ev[1], key=lambda x: x[1])
        return t_ev, t_prev_mm, ev_mc

    def ev_copy(dst_i):
        def f(s, mc, nt, bank, dst_i=dst_i):
            return s.copy(out=abf(dst_i, mc)[:, nt * 512:(nt + 1) * 512], in_=bank)
        return f

    def ev_tanh(dst_i):
        def f(s, mc, nt, bank, dst_i=dst_i):
            return s.activation(out=abf(dst_i, mc)[:, nt * 512:(nt + 1) * 512],
                                in_=bank, func=AF.Tanh)
        return f

    trecv_free = [None]
    t_res_ok = TOK_XN
    prev_layer = TOK_XNPARK
    tmm = None
    t_hsend_free = None
    t_tots_free = None

    for l in range(L):
        t_pv, t_pw = emit_params(l, [prev_layer])
        t_mu = emit_stats(res_c, [t_res_ok] + ([TOK_XNPARK] if l == 0 else []))
        t_g = emit_gfast(t_mu)
        def f_hs(v):
            last = None
            for c in range(CH):
                last = v.tensor_copy(out=HSEND[:, c:c + 1], in_=AR[2][:, c * T + T - 1:c * T + T])
            return last
        t_hs = o("v", f_hs, after=[t_g] + ([t_hsend_free] if t_hsend_free else []), inc="V", par=True)
        t_prevc, t_hsend_free = emit_halo_ag(l, "hg", t_hs)
        t_park = None
        for c in range(CH):
            t_park = dma("y", res_d[c * 128:(c + 1) * 128, :], res_c(c), after=[t_g])
        t_mk = emit_mixed(3, P["ak"], P["bk"], [t_g, t_pv], after_col0=[t_prevc])
        t_mv = emit_mixed(4, P["av"], P["bv"], [t_mk])
        t_mr = emit_mixed(0, P["arr"], P["brr"], [t_mv, t_park])
        t_k, tmm, kev = emit_cc_proj(f"wk{l}", 3, ev_copy(1), t_mk, tmm, evac_extra=[t_park])
        t_v_, tmm, vev = emit_cc_proj(f"wv{l}", 4, ev_copy(3), t_mv, tmm, evac_extra=[t_k])

        DRV = ar_f32(4)[:, 0:T]
        t_tot = None
        for c in range(CH):
            t_dv = o("s", (lambda s, c=c:
                           s.activation(out=DRV, in_=TROW, func=AF.Exp,
                                        scale=P["nw"][:, c:c + 1], bias=P["b2k"][:, c:c + 1])),
                     after=[t_pw, tmm] + ([t_tot] if t_tot else []), inc="S")
            t_ek = o("s", (lambda s, c=c:
                           s.activation(out=hf32(0), in_=AR[1][:, c * T:(c + 1) * T],
                                        func=AF.Exp, bias=cpc(-DELTA))),
                     after=[kev[c]], inc="S")
            t_ekv = o("v", (lambda v, c=c:
                            v.tensor_tensor(out=hf32(1), in0=hf32(0), in1=abf(3, c), op=OP.mult)),
                      after=[t_ek, vev[c]], inc="V")
            def f_t1(v, c=c):
                v.scalar_tensor_tensor(out=hf32(1), in0=DRV, scalar=1.0, in1=hf32(1),
                                       op0=OP.mult, op1=OP.mult, accum_out=TOTS[:, c:c + 1])
                return v.scalar_tensor_tensor(out=hf32(0), in0=DRV, scalar=1.0, in1=hf32(0),
                                              op0=OP.mult, op1=OP.mult, accum_out=TOTS[:, 8 + c:9 + c])
            aft = [t_ekv, t_dv]
            if c == 0 and t_tots_free is not None:
                aft.append(t_tots_free)
            t_tot = o("v", f_t1, after=aft, inc="V")

        t_ti = dma("y", ccb[f"tt{l}_i"], TOTS, after=[t_tot])
        t_tots_free = t_ti
        def f_cc(g, l=l):
            return g.collective_compute("AllGather", OP.bypass,
                                        replica_groups=[list(range(NCORE))],
                                        ins=[ccb[f"tt{l}_i"].opt()], outs=[ccb[f"tt{l}_o"].opt()])
        t_cc = o("g", f_cc, after=[t_ti], inc="CC")
        t_tr = dma("y", TRECV.rearrange("p (r c) -> p r c", r=8),
                   ccb[f"tt{l}_o"].rearrange("(r p) c -> p r c", p=128),
                   after=[t_cc] + ([trecv_free[0]] if trecv_free[0] else []))
        t_rt, tmm, _rev = emit_cc_proj(f"wr{l}", 0, ev_tanh(4), t_mr, tmm,
                                       evac_extra=[t_tot])
        t_pk = dma("g", wckb_d[l], dram[f"wck{l}"])
        t_pv2 = dma("g", wcvb_d[l], dram[f"wcv{l}"])
        PREPTOK = t_pv2
        def f_w16(v):
            v.tensor_copy(out=W16[:, 0:8], in_=P["wdec"])
            v.tensor_copy(out=W16[:, 8:16], in_=P["wdec"])
            last = None
            for r_ in range(8):
                last = v.tensor_copy(out=WREP[:, r_ * 16:(r_ + 1) * 16], in_=W16)
            return last
        t_wrep = o("v", f_w16, after=[t_pw], inc="V")
        t_c1 = o("v", (lambda v: v.tensor_tensor(out=COEF, in0=WREP, in1=MTS, op=OP.mult)),
                 after=[t_wrep], inc="V")
        t_c2 = o("s", (lambda s: s.activation(out=COEF, in_=COEF, func=AF.Exp)),
                 after=[t_c1], inc="S")
        def f_c3(v, l=l):
            v.tensor_tensor(out=COEF, in0=COEF, in1=MTM, op=OP.mult)
            v.tensor_tensor(out=COEF, in0=COEF, in1=TRECV, op=OP.mult)
            return v.reduce_sum(out=ABINIT, in_=COEF.rearrange("p (r c) -> p c r", r=8),
                                axis=AX)
        t_abi = o("v", f_c3, after=[t_c2, t_tr], inc="V")
        trecv_free[0] = t_abi

        if l == 2:
            t_sc = t_abi
            for c in range(CH):
                t_eb = o("s", (lambda s, c=c:
                               s.activation(out=EWB, in_=TROW, func=AF.Exp,
                                            scale=cpc(0.0), bias=P["wdec"][:, c:c + 1])),
                         after=[t_pw, t_sc, tmm], inc="S")
                t_ek2 = o("s", (lambda s, c=c:
                                s.activation(out=hf32(0), in_=AR[1][:, c * T:(c + 1) * T],
                                             func=AF.Exp, bias=cpc(-DELTA))),
                          inc="S")
                t_ekv2 = o("v", (lambda v, c=c:
                                 v.tensor_tensor(out=hf32(1), in0=hf32(0), in1=abf(3, c), op=OP.mult)),
                           after=[t_ek2, t_abi], inc="V")
                def f_scan(v, c=c):
                    v.tensor_copy(out=ASH[:, 0:1], in_=ABINIT[:, c:c + 1])
                    v.tensor_copy(out=BSH[:, 0:1], in_=ABINIT[:, 8 + c:9 + c])
                    v.tensor_tensor_scan(out=ASH[:, 1:T + 1], data0=EWB, data1=hf32(1),
                                         initial=ABINIT[:, c:c + 1], op0=OP.mult, op1=OP.add)
                    return v.tensor_tensor_scan(out=BSH[:, 1:T + 1], data0=EWB, data1=hf32(0),
                                                initial=ABINIT[:, 8 + c:9 + c], op0=OP.mult, op1=OP.add)
                t_scan = o("v", f_scan, after=[t_ekv2, t_eb], inc="V", par=True)
                def f_nd(v, c=c):
                    v.scalar_tensor_tensor(out=hf32(1), in0=hf32(1), scalar=P["eu"][:, c:c + 1],
                                           in1=ASH[:, 0:T], op0=OP.mult, op1=OP.add)
                    return v.scalar_tensor_tensor(out=hf32(0), in0=hf32(0), scalar=P["eu"][:, c:c + 1],
                                                  in1=BSH[:, 0:T], op0=OP.mult, op1=OP.add)
                t_nd = o("v", f_nd, after=[t_scan], inc="V", par=True)
                t_ln = o("s", (lambda s: (s.activation(out=EWB, in_=hf32(0), func=AF.Ln),
                                          s.activation(out=EWB, in_=EWB, func=AF.Exp, scale=cpc(-1.0)))[-1]),
                         after=[t_nd], inc="S")
                def f_wkv(v, c=c):
                    v.tensor_tensor(out=hf32(1), in0=hf32(1), in1=EWB, op=OP.mult)
                    return v.scalar_tensor_tensor(out=abf(3, c), in0=abf(4, c), scalar=1.0,
                                                  in1=hf32(1), op0=OP.add, op1=OP.mult)
                t_sc = o("v", f_wkv, after=[t_ln, t_rt], inc="V")
            TOK_GATED = t_sc
        else:
            A0F = ar_f32(0)
            A2F = ar_f32(2)
            BUF = [
                dict(EW=A0F[:, 2 * T:3 * T], EK=hf32(0), EKV=hf32(1),
                     ASHX=A0F[:, 0:T], BSHX=A0F[:, T:2 * T]),
                dict(EW=A2F[:, 2 * T:3 * T], EK=A2F[:, 3 * T:4 * T],
                     EKV=A0F[:, 3 * T:4 * T],
                     ASHX=A2F[:, 0:T], BSHX=A2F[:, T:2 * T]),
            ]
            t_eb = {}; t_ek2 = {}; t_nd = {}; t_ln = {}; t_wkv = {}
            first_aft = [t_pw, t_abi, tmm, t_mr]

            def em_s(c):
                b = BUF[c % 2]
                aft = list(first_aft) if c < 2 else []
                t_eb[c] = o("s", (lambda s, c=c, b=b:
                                  s.activation(out=b["EW"], in_=TROW, func=AF.Exp,
                                               scale=cpc(0.0), bias=P["wdec"][:, c:c + 1])),
                            after=aft + ([t_wkv[c - 2]] if c >= 2 else []), inc="S")
                t_ek2[c] = o("s", (lambda s, c=c, b=b:
                                   s.activation(out=b["EK"], in_=AR[1][:, c * T:(c + 1) * T],
                                                func=AF.Exp, bias=cpc(-DELTA))),
                             after=([t_nd[c - 2]] if c >= 2 else []), inc="S")

            def em_v(c):
                b = BUF[c % 2]
                aft = list(first_aft) if c < 2 else []
                t_ekv2 = o("g", (lambda g, c=c, b=b:
                                 g.tensor_tensor(out=b["EKV"], in0=b["EK"], in1=abf(3, c), op=OP.mult)),
                           after=aft + [t_ek2[c]] + ([t_wkv[c - 2]] if c >= 2 else []), inc="G")
                def f_scan(v, c=c, b=b):
                    v.tensor_tensor_scan(out=b["ASHX"], data0=b["EW"], data1=b["EKV"],
                                         initial=ABINIT[:, c:c + 1], op0=OP.mult, op1=OP.add)
                    return v.tensor_tensor_scan(out=b["BSHX"], data0=b["EW"], data1=b["EK"],
                                                initial=ABINIT[:, 8 + c:9 + c], op0=OP.mult, op1=OP.add)
                t_scan = o("v", f_scan, after=[t_ekv2, t_eb[c]], inc="V", par=True)
                def f_nd(v, c=c, b=b):
                    v.scalar_tensor_tensor(out=b["EKV"][:, 0:1], in0=b["EKV"][:, 0:1],
                                           scalar=P["eu"][:, c:c + 1],
                                           in1=ABINIT[:, c:c + 1], op0=OP.mult, op1=OP.add)
                    v.scalar_tensor_tensor(out=b["EKV"][:, 1:T], in0=b["EKV"][:, 1:T],
                                           scalar=P["eu"][:, c:c + 1],
                                           in1=b["ASHX"][:, 0:T - 1], op0=OP.mult, op1=OP.add)
                    v.scalar_tensor_tensor(out=b["EK"][:, 0:1], in0=b["EK"][:, 0:1],
                                           scalar=P["eu"][:, c:c + 1],
                                           in1=ABINIT[:, 8 + c:9 + c], op0=OP.mult, op1=OP.add)
                    return v.scalar_tensor_tensor(out=b["EK"][:, 1:T], in0=b["EK"][:, 1:T],
                                                  scalar=P["eu"][:, c:c + 1],
                                                  in1=b["BSHX"][:, 0:T - 1], op0=OP.mult, op1=OP.add)
                t_nd[c] = o("v", f_nd, after=[t_scan], inc="V", par=True)

            def em_ln(c):
                b = BUF[c % 2]
                t_ln[c] = o("s", (lambda s, b=b:
                                  (s.activation(out=b["EW"], in_=b["EK"], func=AF.Ln),
                                   s.activation(out=b["EW"], in_=b["EW"], func=AF.Exp,
                                                scale=cpc(-1.0)))[-1]),
                            after=[t_nd[c]], inc="S")

            def em_wkv(c):
                b = BUF[c % 2]
                t_wm = o("g", (lambda g, b=b:
                               g.tensor_tensor(out=b["EKV"], in0=b["EKV"], in1=b["EW"], op=OP.mult)),
                         after=[t_ln[c]], inc="G")
                t_wkv[c] = o("v", (lambda v, c=c, b=b:
                                   v.scalar_tensor_tensor(out=abf(3, c), in0=abf(4, c), scalar=1.0,
                                                          in1=b["EKV"], op0=OP.add, op1=OP.mult)),
                             after=[t_wm, t_rt], inc="V")

            for c in range(CH):
                em_s(c)
                if c >= 1:
                    em_ln(c - 1)
                em_v(c)
                if c >= 1:
                    em_wkv(c - 1)
            em_ln(CH - 1)
            em_wkv(CH - 1)
            TOK_GATED = t_wkv[CH - 1]
        if l == 2:
            emit_ffn([t_rt, TOK_GATED], t_mr)
            tmm = FFN_STATE["mm"]

        t_rl = None
        for c in range(CH):
            t_rl = dma("y", res_c(c), res_d[c * 128:(c + 1) * 128, :], after=[TOK_GATED])
        pieces = [load_w_piece(dram[f"wo{l}"][0], cast=True),
                  load_w_piece(dram[f"wo{l}"][1], cast=True)]
        ev = {0: None, 1: None}
        t_prev_mm = tmm
        for mc in range(CH):
            grp = mc % 2
            banks = PB[4 * grp:4 * grp + 4]
            while len(pieces) <= mc // 2:
                h = len(pieces)
                pieces.append(load_w_piece(dram[f"wo{l}"][h], cast=True))
            bi, t_wp = pieces[mc // 2]
            aft = [TOK_GATED, t_wp, t_rl]
            if l == 2 and mc < 2:
                aft.append(FFN_STATE["banks"])
            if t_prev_mm is not None:
                aft.append(t_prev_mm)
            if ev[grp] is not None:
                aft.append(ev[grp])
            def f_mm(t, mc=mc, banks=banks, bi=bi):
                last = None
                for kc in range(CH):
                    w_ = wsl(bi, kc, (mc % 2) * 128)
                    for nt in range(NT):
                        last = t.matmul(banks[nt], w_, abf(3, kc)[:, nt * 512:(nt + 1) * 512],
                                        start=(kc == 0), stop=(kc == CH - 1))
                return last
            t_mm = o("t", f_mm, after=aft, inc="T")
            wb_busy[bi] = t_mm
            def f_ev(v, mc=mc, banks=banks):
                last = None
                for nt in range(NT):
                    sl = slice(nt * 512, (nt + 1) * 512)
                    last = v.tensor_tensor(out=res_c(mc)[:, sl], in0=res_c(mc)[:, sl],
                                           in1=banks[nt], op=OP.add)
                return last
            ev[grp] = o("v", f_ev, after=[t_mm], inc="V", par=True)
            t_prev_mm = t_mm
        t_res_ok = max(ev[0], ev[1], key=lambda x: x[1])
        tmm = t_prev_mm

        t_mu2 = emit_stats(res_c, [t_res_ok] +
                           ([FFN_STATE["acc"], FFN_STATE["banks"], FFN_STATE["mm"]] if l == 2 else []))
        t_g2 = emit_gfast(t_mu2)
        def f_hs2(v):
            last = None
            for c in range(CH):
                last = v.tensor_copy(out=HSEND[:, c:c + 1], in_=AR[2][:, c * T + T - 1:c * T + T])
            return last
        t_hs2 = o("v", f_hs2, after=[t_g2] + ([t_hsend_free] if t_hsend_free else []), inc="V", par=True)
        t_prevc2, t_hsend_free = emit_halo_ag(l, "h2", t_hs2)
        t_m2k = emit_mixed(3, P["ack"], P["bck"], [t_g2, t_pv], after_col0=[t_prevc2])
        t_m2r = emit_mixed(4, P["acr"], P["bcr"], [t_m2k])
        t_rt2, tmm, _r2ev = emit_cc_proj(f"wcr{l}", 4, ev_tanh(2), t_m2r, tmm, evac_extra=[t_m2r])

        qlast = [None]
        t_parkcm = None
        if l < 2:
            for c in range(CH):
                t_parkcm = dma("y", res_d[c * 128:(c + 1) * 128, :], res_c(c),
                               after=[t_mu2])
        t_cm = tmm
        t_evk_all = t_rt2
        t_kkdone = {}
        for nt in range(NT):
            t_evk = None
            evk_hist = {}
            for fg in range(16):
                bi_k, t_wk = load_w_piece(wckb_d[l][fg], cast=False, after=[PREPTOK])
                for fi in range(2):
                    f_ = fg * 2 + fi
                    bank = PB[f_ % 4]
                    aft = [t_m2k, t_wk]
                    if t_cm is not None and f_ == 0:
                        aft.append(t_cm)
                    if evk_hist.get(f_ - 4) is not None:
                        aft.append(evk_hist[f_ - 4])
                    def f_kk(t, fi=fi, bank=bank, bi_k=bi_k, nt=nt):
                        last = None
                        for kc in range(CH):
                            w_ = wsl(bi_k, kc, fi * 128)
                            last = t.matmul(bank, w_, abf(3, kc)[:, nt * 512:(nt + 1) * 512],
                                            start=(kc == 0), stop=(kc == CH - 1))
                        return last
                    t_kk = o("t", f_kk, after=aft, inc="T")
                    wb_busy[bi_k] = t_kk
                    slot = f_ % 2
                    JB = JUNK.bitcast(BF16)
                    def f_rl(s, bank=bank, slot=slot):
                        return s.activation(out=JB[:, slot * 512:(slot + 1) * 512],
                                            in_=bank, func=AF.Relu)
                    t_rl1 = o("s", f_rl, after=[t_kk] + ([t_evk_all] if (nt == 0 and fg == 0 and fi == 0) else [])
                              + ([t_evk] if t_evk else []), inc="S")
                    def f_rq(s, f_=f_, slot=slot):
                        return s.activation(out=AR[4][:, f_ * 512:(f_ + 1) * 512],
                                            in_=JB[:, slot * 512:(slot + 1) * 512],
                                            func=AF.Square)
                    t_evk = o("s", f_rq, after=[t_rl1], inc="S")
                    evk_hist[f_] = t_evk
                    t_cm = t_kk
            t_kkdone[nt] = t_evk
            t_vv = None
            for fg in range(16):
                bi_v, t_wv = load_w_piece(wcvb_d[l][fg], cast=False, after=[PREPTOK])
                aft = [t_evk, t_wv]
                if t_cm is not None:
                    aft.append(t_cm)
                def f_vv(t, fg=fg, bi_v=bi_v):
                    last = None
                    for fi in range(2):
                        f_ = fg * 2 + fi
                        for mc in range(CH):
                            w_ = WB[bi_v][:, fi * 1024 + mc * 128:fi * 1024 + mc * 128 + 128]
                            last = t.matmul(PB[mc], w_, AR[4][:, f_ * 512:(f_ + 1) * 512],
                                            start=(f_ == 0), stop=(f_ == FCH - 1))
                    return last
                t_vv = o("t", f_vv, after=aft, inc="T")
                wb_busy[bi_v] = t_vv
                t_cm = t_vv
            if l < 2:
                def f_vev(s, nt=nt):
                    last = None
                    for mc in range(CH):
                        last = s.copy(out=abf(3, mc)[:, nt * 512:(nt + 1) * 512], in_=PB[mc])
                    return last
                t_cm = o("s", f_vev, after=[t_vv, t_kkdone[nt]], inc="S", par=True)
                qc = 4 * l + nt
                tq6, tqout = emit_quant_chunk(qc, [t_parkcm, TOK_GATED])
                qlast[0] = tqout
            else:
                for mc in range(CH):
                    def f_gt(v, mc=mc, nt=nt):
                        sl = slice(nt * 512, (nt + 1) * 512)
                        v.scalar_tensor_tensor(out=JUNK,
                                               in0=AR[2][:, mc * T + nt * 512:mc * T + (nt + 1) * 512],
                                               scalar=1.0, in1=PB[mc], op0=OP.add, op1=OP.mult)
                        return v.tensor_tensor(out=res_c(mc)[:, sl], in0=res_c(mc)[:, sl],
                                               in1=JUNK[:, 0:512], op=OP.add)
                    t_cm = o("v", f_gt, after=[t_vv, t_rt2], inc="V")
            t_evk_all = t_cm
        if l < 2:
            t_rl2 = None
            for c in range(CH):
                t_rl2 = dma("y", res_c(c), res_d[c * 128:(c + 1) * 128, :],
                            after=[qlast[0], t_cm])
            def f_gv(v):
                last = None
                for mc in range(CH):
                    last = v.scalar_tensor_tensor(out=AR[3][:, mc * T:(mc + 1) * T],
                                                  in0=AR[2][:, mc * T:(mc + 1) * T],
                                                  scalar=1.0, in1=AR[3][:, mc * T:(mc + 1) * T],
                                                  op0=OP.add, op1=OP.mult)
                return last
            t_gv = o("v", f_gv, after=[t_cm, t_rt2], inc="V", par=True)
            def f_ra(v):
                last = None
                for mc in range(CH):
                    last = v.tensor_tensor(out=res_c(mc), in0=res_c(mc),
                                           in1=AR[3][:, mc * T:(mc + 1) * T], op=OP.add)
                return last
            t_res_ok = o("v", f_ra, after=[t_gv, t_rl2], inc="V", par=True)
        else:
            t_res_ok = t_cm
        prev_layer = t_res_ok

    t_muf = emit_stats(res_c, [t_res_ok])
    def accslot(c):
        return ar_f32(3 + c // 4)[:, (c % 4) * T:(c % 4) * T + T]
    t_al = None
    for c in range(CH):
        t_al = dma("y", accslot(c), acc_d[c * 128:(c + 1) * 128, :],
                   after=[FFN_STATE["acc"], t_res_ok])
    HF = HALF.bitcast(F32)
    t_f2 = {}
    t_out_last = None
    for nt in range(NT):
        sl = slice(nt * 512, (nt + 1) * 512)
        pair = (6, 7) if nt % 2 == 0 else (4, 5)
        t_b = emit_bcast(nt, [t_muf] + ([t_f2[nt - 2]] if (nt - 2) in t_f2 else []), pair=pair)
        def f1(v, sl=sl, pair=pair):
            last = None
            for c in range(CH):
                last = v.tensor_tensor(out=HF[:, c * 512:c * 512 + 512],
                                       in0=res_c(c)[:, sl], in1=PB[pair[0]], op=OP.mult)
            return last
        t_1 = o("v", f1, after=[t_b, t_al], inc="V", par=True)
        def f2(v, sl=sl, pair=pair):
            last = None
            for c in range(CH):
                last = v.tensor_tensor(out=HF[:, c * 512:c * 512 + 512],
                                       in0=HF[:, c * 512:c * 512 + 512],
                                       in1=PB[pair[1]], op=OP.subtract)
            return last
        t_2 = o("v", f2, after=[t_1], inc="V", par=True)
        def f3(v, sl=sl):
            last = None
            for c in range(CH):
                last = v.scalar_tensor_tensor(out=accslot(c)[:, sl],
                                              in0=HF[:, c * 512:c * 512 + 512],
                                              scalar=P["lnfw"][:, c:c + 1],
                                              in1=accslot(c)[:, sl],
                                              op0=OP.mult, op1=OP.add)
            return last
        t_3 = o("v", f3, after=[t_2], inc="V", par=True)
        t_f2[nt] = t_3
    for c in range(CH):
        t_out_last = dma("y", out_d[c * 128:(c + 1) * 128, :], accslot(c),
                         after=[t_f2[NT - 1]])
    o("y", lambda e: None, after=[t_out_last])

    stack = contextlib.ExitStack()
    sems = {}
    for n in ("T", "V", "S", "G", "GD", "Y", "CC", "VX", "SX", "GX"):
        sems[n] = stack.enter_context(nc.semaphore("sem" + n))

    COMPUTE_OPS = {
        "tensor_tensor", "tensor_add", "tensor_sub", "tensor_mul", "tensor_max",
        "tensor_scalar", "tensor_single_scalar", "tensor_scalar_add",
        "tensor_scalar_sub", "tensor_scalar_mul", "tensor_scalar_max",
        "tensor_scalar_min", "tensor_relu", "tensor_reduce", "reduce_max",
        "reduce_sum", "scalar_tensor_tensor", "tensor_tensor_scan",
        "tensor_copy", "memset", "memzero", "reciprocal", "select", "transpose",
        "activation", "sign", "square", "sqrt", "mul", "copy", "add",
    }

    class SerialEng:

        def __init__(self, eng, xsem):
            self._eng = eng
            self._x = xsem
            self._xcnt = [0]
            self._pending = [None]
            self._last_sig = [None]
            self._par = [False]
            self._par_first = [False]

        def _seal_pending_x(self):
            if self._pending[0] is not None:
                self._xcnt[0] += 1
                self._pending[0].then_inc(self._x, 1)
                self._last_sig[0] = (self._x, self._xcnt[0])
                self._pending[0] = None

        def seal_op(self, sem, amt, val):
            if self._pending[0] is None:
                self._pending[0] = self._eng.nop()
            self._pending[0].then_inc(sem, amt)
            self._pending[0] = None
            self._last_sig[0] = (sem, val)

        def __getattr__(self, name):
            real = getattr(self._eng, name)
            if name not in COMPUTE_OPS:
                return real

            def call(*a, **kw):
                self._seal_pending_x()
                if self._last_sig[0] is not None and (not self._par[0] or self._par_first[0]):
                    s_, v_ = self._last_sig[0]
                    self._eng.wait_ge(s_, v_)
                self._par_first[0] = False
                inst = real(*a, **kw)
                self._pending[0] = inst
                return inst
            return call

        def begin_par(self):
            self._par[0] = True
            self._par_first[0] = True

        def end_par(self):
            self._par[0] = False

    with nc.Block() as block:
        def emitter(engname):
            serial = engname in ("v", "s", "g")
            xsem = {"v": "VX", "s": "SX", "g": "GX"}.get(engname)

            def run(eng):
                we = SerialEng(eng, sems[xsem]) if serial else eng
                for fn, waits, tok, amt, par in sched.stream[engname]:
                    for s_, v_ in waits:
                        eng.wait_ge(sems[s_], v_)
                    if par and serial:
                        we.begin_par()
                    inst = fn(we)
                    if par and serial:
                        we.end_par()
                    if tok is not None:
                        if tok[0] in ("GD", "Y", "CC"):
                            inst.then_inc(sems[tok[0]], amt)
                        elif serial:
                            we.seal_op(sems[tok[0]], amt, tok[1])
                        else:
                            if inst is None:
                                inst = eng.nop()
                            inst.then_inc(sems[tok[0]], amt)
                if serial:
                    we._seal_pending_x()
            return run
        block.tensor(emitter("t"))
        block.vector(emitter("v"))
        block.scalar(emitter("s"))
        block.gpsimd(emitter("g"))
        block.sync(emitter("y"))
    stack.close()
    return nc


_ncache = {}


def kernel(**inputs):
    x = np.asarray(inputs["L0_attn_x"], np.float32).reshape(16384, 1024)
    xT = np.ascontiguousarray(x.T)

    def colpack(v):
        return np.ascontiguousarray(np.asarray(v, np.float32).reshape(CH, 128).T)

    def piece_pack(wT, pw):
        K, O = wT.shape
        kch, npc = K // 128, O // pw
        a = wT.reshape(kch, 128, npc, pw)
        return np.ascontiguousarray(a.transpose(2, 1, 0, 3).reshape(npc, 128, kch * pw))

    def fpack(wT):
        a = wT.reshape(16, 2, 128, 1024)
        return np.ascontiguousarray(a.transpose(0, 2, 1, 3).reshape(16, 128, 2048))

    common = {}
    for l in range(L):
        common[f"wk{l}"] = piece_pack(np.asarray(inputs["L0_attn_tm_k_w"][l], np.float32).T, 256)
        common[f"wv{l}"] = piece_pack(np.asarray(inputs["L0_attn_tm_v_w"][l], np.float32).T, 256)
        common[f"wr{l}"] = piece_pack(np.asarray(inputs["L0_attn_tm_r_w"][l], np.float32).T, 256)
        common[f"wo{l}"] = piece_pack(np.asarray(inputs["L0_attn_tm_o_w"][l], np.float32).T, 256)
        common[f"wck{l}"] = piece_pack(np.asarray(inputs["L0_attn_cm_k_w"][l], np.float32).T, 256)
        common[f"wcv{l}"] = fpack(np.asarray(inputs["L0_attn_cm_v_w"][l], np.float32).T)
        common[f"wcr{l}"] = piece_pack(np.asarray(inputs["L0_attn_cm_r_w"][l], np.float32).T, 256)
        common[f"ln1w{l}"] = colpack(inputs["L0_attn_ln1_w"][l])
        common[f"mxk{l}"] = colpack(np.asarray(inputs["L0_attn_tm_mix_k"])[l].reshape(-1))
        common[f"mxv{l}"] = colpack(np.asarray(inputs["L0_attn_tm_mix_v"])[l].reshape(-1))
        common[f"mxr{l}"] = colpack(np.asarray(inputs["L0_attn_tm_mix_r"])[l].reshape(-1))
        common[f"td{l}"] = colpack(inputs["L0_attn_time_decay"][l])
        common[f"tf{l}"] = colpack(inputs["L0_attn_time_first"][l])
        common[f"ln2w{l}"] = colpack(inputs["L0_attn_ln2_w"][l])
        common[f"cmk{l}"] = colpack(np.asarray(inputs["L0_attn_cm_mix_k"])[l].reshape(-1))
        common[f"cmr{l}"] = colpack(np.asarray(inputs["L0_attn_cm_mix_r"])[l].reshape(-1))
    common["w8"] = piece_pack(np.asarray(inputs["L0_ffn_weight_fp8"], np.float32).T, 256)
    common["lnfw"] = colpack(inputs["L0_attn_lnf_w"])
    common["fbias"] = colpack(inputs["L0_ffn_bias"])
    common["scol"] = np.ascontiguousarray(
        np.broadcast_to(np.asarray(inputs["L0_ffn_input_scale"], np.float32).reshape(1, 1), (128, 1)))
    common["wscol"] = np.ascontiguousarray(
        np.broadcast_to(np.asarray(inputs["L0_ffn_weight_scale"], np.float32).reshape(1, 1), (128, 1)))

    statlh = np.zeros((128, 16), BF)
    for nt in range(NT):
        statlh[:, 4 * nt + nt] = BF(1.0)
    common["statlh"] = statlh
    eyeb = np.zeros((4, 512), np.float32)
    for nt in range(NT):
        eyeb[nt, 128 * nt:128 * (nt + 1)] = 1.0
    common["eyeb"] = eyeb
    common["trow"] = np.ascontiguousarray(
        np.broadcast_to(np.arange(T, dtype=np.float32)[None, :], (128, T)))

    in_maps = []
    for i in range(NCORE):
        m = dict(common)
        m["x"] = np.ascontiguousarray(xT[:, i * T:(i + 1) * T])
        mh = np.zeros((128, 8, 8), np.float32)
        if i > 0:
            mh[:, i - 1, :] = 1.0
        m["maskh"] = np.ascontiguousarray(mh.reshape(128, 64))
        steps = np.zeros((128, 8, 16), np.float32)
        mask = np.zeros((128, 8, 16), np.float32)
        for r in range(i):
            steps[:, r, :] = 2048.0 * (i - 1 - r)
            mask[:, r, :] = 1.0
        m["mtot_steps"] = np.ascontiguousarray(steps.reshape(128, 128))
        m["mtot_mask"] = np.ascontiguousarray(mask.reshape(128, 128))
        in_maps.append(m)

    if "nc" not in _ncache:
        _ncache["nc"] = build_program()
    nc = _ncache["nc"]
    res = run_bass_kernel_spmd(nc, in_maps, core_ids=list(range(NCORE)))
    outT = np.concatenate([np.asarray(res.results[i]["o"], np.float32) for i in range(NCORE)], axis=1)
    return np.ascontiguousarray(outT.T).reshape(1, 16384, 1024)


# version 20 (final)
# speedup vs baseline: 1.0590x; 1.0590x over previous
import contextlib
import numpy as np
import ml_dtypes
import concourse.bass as bass
import concourse.mybir as mybir
from concourse.bass_utils import run_bass_kernel_spmd

F32 = mybir.dt.float32
BF16 = mybir.dt.bfloat16
AF = mybir.ActivationFunctionType
OP = mybir.AluOpType
BF = ml_dtypes.bfloat16

NCORE = 8
C = 1024
CH = 8
T = 2048
NT = 4
L = 4
FFN = 4096
FCH = 32
DELTA = 4.0
EPS = 1e-5
LN2 = float(np.log(np.float32(2.0)))
R5 = float(np.sqrt(np.float32(0.5)))
AX = mybir.AxisListType.X

ENG = ("t", "v", "s", "g", "y")


class Sched:
    def __init__(self, semnames):
        self.stream = {e: [] for e in ENG}
        self.cnt = {n: 0 for n in semnames}
        self.lastwait = {e: {n: 0 for n in semnames} for e in ENG}

    def op(self, eng, fn, after=(), inc=None, amt=1, par=False):
        waits = {}
        for tk in after:
            if tk is None:
                continue
            s, v = tk
            if v > self.lastwait[eng].get(s, 0):
                waits[s] = max(waits.get(s, 0), v)
        for s, v in waits.items():
            self.lastwait[eng][s] = v
        tok = None
        if inc is not None:
            self.cnt[inc] += amt
            tok = (inc, self.cnt[inc])
            self.lastwait[eng][inc] = self.cnt[inc]
        self.stream[eng].append((fn, sorted(waits.items()), tok, amt, par))
        return tok


def build_program():
    nc = bass.Bass("TRN2", target_bir_lowering=False, detect_race_conditions=False)
    dram = {}

    def din(name, shape, dt=F32):
        dram[name] = nc.dram_tensor(name, shape, dt, kind="ExternalInput").ap()
        return dram[name]

    x_in = din("x", [C, T])
    for l in range(L):
        din(f"wk{l}", [4, 128, 2048]); din(f"wv{l}", [4, 128, 2048])
        din(f"wr{l}", [4, 128, 2048]); din(f"wo{l}", [4, 128, 2048])
        din(f"wck{l}", [16, 128, 2048]); din(f"wcv{l}", [16, 128, 2048])
        din(f"wcr{l}", [4, 128, 2048])
        for p in ("ln1w", "mxk", "mxv", "mxr", "td", "tf", "ln2w", "cmk", "cmr"):
            din(f"{p}{l}", [128, CH])
    din("w8", [4, 128, 2048])
    din("lnfw", [128, CH]); din("fbias", [128, CH])
    din("scol", [128, 1]); din("wscol", [128, 1])
    din("statlh", [128, 16], BF16)
    din("eyeb", [4, 512])
    din("trow", [128, T])
    din("maskh", [128, 64])
    din("mtot_steps", [128, 128])
    din("mtot_mask", [128, 128])

    out_d = nc.dram_tensor("o", [C, T], F32, kind="ExternalOutput").ap()
    wckb_d = [nc.dram_tensor(f"wckb{l}", [16, 128, 2048], BF16).ap() for l in range(L)]
    wcvb_d = [nc.dram_tensor(f"wcvb{l}", [16, 128, 2048], BF16).ap() for l in range(L)]
    acc_d = nc.dram_tensor("acc_d", [C, T], F32).ap()
    res_d = nc.dram_tensor("res_d", [C, T], F32).ap()
    ccb = {}
    for l in range(L):
        for nm, w in (("hg", 8), ("h2", 8), ("tt", 16)):
            ccb[f"{nm}{l}_i"] = nc.dram_tensor(f"{nm}{l}_i", [128, w], F32).ap()
            ccb[f"{nm}{l}_o"] = nc.dram_tensor(f"{nm}{l}_o", [128 * NCORE, w], F32).ap()

    sbt = lambda n, s, dt=F32: nc.alloc_sbuf_tensor(n, s, dt).ap()
    AR = [sbt(f"ar{i}", [128, 16384], BF16) for i in range(5)]
    HALF = sbt("half", [128, 8192], BF16)
    WB = [sbt(f"wb{i}", [128, 2048], BF16) for i in range(2)]
    TROW = sbt("trow_s", [128, T])
    STATLH = sbt("statlh_s", [128, 16], BF16)
    EYEB = sbt("eyeb_s", [4, 512])
    MASKH = sbt("maskh_s", [128, 64])
    MTS = sbt("mts_s", [128, 128])
    MTM = sbt("mtm_s", [128, 128])
    ST1 = sbt("st1", [4, 512]); ST2F = sbt("st2", [128, 512]); STS = sbt("sts", [4, 512])
    ST2 = ST2F[0:4, :]
    CP = sbt("cpool", [128, 16])
    pnames = ("ln1w", "mxk", "mxv", "mxr", "td", "tf", "ln2w", "cmk", "cmr",
              "ak", "bk", "av", "bv", "arr", "brr", "ack", "bck", "acr", "bcr",
              "wdec", "eu", "lnfw", "fbias", "nw", "b2k")
    P = {n: sbt("p_" + n, [128, CH]) for n in pnames}
    SCOL = sbt("p_scol", [128, 1]); WSCOL = sbt("p_wscol", [128, 1])
    SINV = sbt("p_sinv", [128, 1]); SWS = sbt("p_sws", [128, 1])
    W16 = sbt("w16", [128, 16])
    WREP = sbt("wrep", [128, 128])
    COEF = sbt("coef", [128, 128])
    TOTS = sbt("tots", [128, 16])
    ABINIT = sbt("abinit", [128, 16])
    HSEND = sbt("hsend", [128, 8])
    HRECV = sbt("hrecv", [128, 64])
    TRECV = sbt("trecv", [128, 128])
    PREVC = sbt("prevc", [128, 8])
    JUNK = sbt("junk", [128, 512])

    PB = [nc.alloc_psum_tensor(f"pb{i}", [128, 512], F32).ap() for i in range(8)]
    ASH = AR[0].bitcast(F32)[:, 0:T + 8]
    BSH = AR[0].bitcast(F32)[:, T + 8:2 * T + 16]
    EWB = AR[0].bitcast(F32)[:, 2 * T + 16:3 * T + 16]

    def ar_f32(i):
        return AR[i].bitcast(F32)

    def res_c(c):
        a = ar_f32(c // 4)
        return a[:, (c % 4) * T:(c % 4) * T + T]

    def abf(i, c):
        return AR[i][:, c * T:(c + 1) * T]

    def hbf(j):
        return HALF[:, j * T:(j + 1) * T]

    def hf32(j):
        return HALF.bitcast(F32)[:, j * T:(j + 1) * T]

    sched = Sched(["T", "V", "S", "G", "GD", "Y", "CC"])
    o = sched.op

    CONSTS = {0: 0.0, 1: 1.0, 2: -1.0, 3: -DELTA, 4: EPS, 5: LN2,
              6: -LN2, 7: -9.0 * LN2, 8: 9.0 * LN2, 9: -0.5}
    def emit_cpool(v):
        last = None
        for j, val in CONSTS.items():
            last = v.memset(CP[:, j:j + 1], val)
        return last
    tok_cp = o("v", emit_cpool, inc="V")
    for j, val in CONSTS.items():
        nc.const_aps.register(F32, val, CP[:, j:j + 1])
    def cpc(val, p=128):
        for j, v in CONSTS.items():
            if v == val:
                return CP[0:p, j:j + 1]
        raise KeyError(val)

    def dma(eng, dst, src, after=()):
        sem = "Y" if eng == "y" else "GD"
        def f(e, dst=dst, src=src):
            return e.dma_start(out=dst, in_=src)
        return o(eng, f, after=after, inc=sem, amt=16)

    for nm, dst in (("trow", TROW), ("statlh", STATLH), ("eyeb", EYEB),
                    ("maskh", MASKH), ("mtot_steps", MTS), ("mtot_mask", MTM),
                    ("scol", SCOL), ("wscol", WSCOL)):
        dma("y", dst, dram[nm])
    dma("y", P["lnfw"], dram["lnfw"])
    dma("y", P["fbias"], dram["fbias"])
    tok_x = None
    for c in range(CH):
        tok_x = dma("y", res_c(c), x_in[c * 128:(c + 1) * 128, :])
    TOK_LOADS = tok_x

    tok_scal = o("v", lambda v: (v.reciprocal(out=SINV, in_=SCOL),
                                 v.tensor_tensor(out=SWS, in0=SCOL, in1=WSCOL, op=OP.mult))[-1],
                 after=[TOK_LOADS, tok_cp], inc="V")


    wb_turn = [0]
    wb_busy = [None, None]

    def load_w_piece(src_ap, cast, after=()):
        i = wb_turn[0]; wb_turn[0] ^= 1
        extra = [wb_busy[i]] if wb_busy[i] is not None else []
        eng = "g" if cast else "y"
        t_ = dma(eng, WB[i], src_ap, after=list(after) + extra)
        return i, t_

    def wsl(i, kc, off, width=128):
        return WB[i][:, kc * 256 + off:kc * 256 + off + width]

    def emit_stats(src_fn, after_toks):
        t_xb = o("s", (lambda s: [s.copy(out=abf(2, c), in_=src_fn(c))
                                  for c in range(CH)][-1]),
                 after=list(after_toks), inc="S", par=True)
        t_prev_mm = None
        mm_hist = {}
        for c in range(CH):
            slot = c % 4
            t_sq = o("s", (lambda s, c=c, slot=slot:
                           s.activation(out=hbf(slot), in_=abf(2, c), func=AF.Square)),
                     after=[t_xb] + ([mm_hist[c - 4]] if (c - 4) in mm_hist else []), inc="S")
            def f_mm(t, c=c, slot=slot):
                last = None
                for nt in range(NT):
                    last = t.matmul(PB[4][0:4, :], STATLH[:, 4 * nt:4 * nt + 4],
                                    abf(2, c)[:, nt * 512:(nt + 1) * 512],
                                    start=(c == 0 and nt == 0), stop=(c == CH - 1 and nt == NT - 1))
                for nt in range(NT):
                    last = t.matmul(PB[5][0:4, :], STATLH[:, 4 * nt:4 * nt + 4],
                                    hbf(slot)[:, nt * 512:(nt + 1) * 512],
                                    start=(c == 0 and nt == 0), stop=(c == CH - 1 and nt == NT - 1))
                return last
            t_prev_mm = o("t", f_mm, after=[t_sq], inc="T")
            mm_hist[c] = t_prev_mm
        def f_stat(v):
            v.tensor_scalar(out=ST1, in0=PB[4][0:4, :], scalar1=1.0 / C, scalar2=None, op0=OP.mult)
            v.tensor_scalar(out=ST2, in0=PB[5][0:4, :], scalar1=1.0 / C, scalar2=None, op0=OP.mult)
            v.tensor_tensor(out=STS, in0=ST1, in1=ST1, op=OP.mult)
            v.tensor_tensor(out=ST2, in0=ST2, in1=STS, op=OP.subtract)
            return v.tensor_scalar(out=ST2, in0=ST2, scalar1=EPS, scalar2=None, op0=OP.add)
        t_v = o("v", f_stat, after=[t_prev_mm], inc="V")
        t_rs = o("s", (lambda s: (s.activation(out=STS, in_=ST2, func=AF.Ln),
                                  s.activation(out=STS, in_=STS, func=AF.Exp, scale=cpc(-0.5, 4)))[-1]),
                 after=[t_v], inc="S")
        t_mu = o("v", (lambda v: v.tensor_tensor(out=ST1, in0=ST1, in1=STS, op=OP.mult)),
                 after=[t_rs], inc="V")
        return t_mu

    def emit_bcast(nt, after_toks, pair=(6, 7)):
        def f(t, nt=nt, pair=pair):
            t.matmul(PB[pair[0]], EYEB[:, 128 * nt:128 * (nt + 1)], STS, start=True, stop=True)
            return t.matmul(PB[pair[1]], EYEB[:, 128 * nt:128 * (nt + 1)], ST1, start=True, stop=True)
        return o("t", f, after=after_toks, inc="T")

    def emit_norm_apply(src_fn, emitfn, t_mu):
        t_prev = None
        for nt in range(NT):
            t_b = emit_bcast(nt, [t_mu] + ([t_prev] if t_prev else []))
            for c in range(CH):
                sl = slice(nt * 512, (nt + 1) * 512)
                def f(v, c=c, sl=sl):
                    return emitfn(v, c, sl)
                t_prev = o("v", f, after=[t_b], inc="V")
        return t_prev

    def emit_gfast(t_mu, first_nt=None, mid=None):
        JB = JUNK.bitcast(BF16)
        JB2 = ST2F.bitcast(BF16)
        t_prev = None
        t_f2 = {}
        order = list(range(NT))
        if first_nt is not None:
            order = [first_nt] + [n for n in range(NT) if n != first_nt]
        for pos, nt in enumerate(order):
            pair = (6, 7) if pos % 2 == 0 else (4, 5)
            jb = JB if pos % 2 == 0 else JB2
            t_b = emit_bcast(nt, [t_mu] + ([t_f2[pos - 2]] if (pos - 2) in t_f2 else []), pair=pair)
            t_e = o("s", (lambda s, jb=jb, pair=pair: (s.copy(out=jb[:, 0:512], in_=PB[pair[0]]),
                                                      s.copy(out=jb[:, 512:1024], in_=PB[pair[1]]))[-1]),
                    after=[t_b], inc="S")
            def f1(v, nt=nt, jb=jb):
                last = None
                for c in range(CH):
                    sl = slice(nt * 512, (nt + 1) * 512)
                    last = v.tensor_tensor(out=abf(2, c)[:, sl], in0=abf(2, c)[:, sl],
                                           in1=jb[:, 0:512], op=OP.mult)
                return last
            t_1 = o("v", f1, after=[t_e], inc="V", par=True)
            def f2(v, nt=nt, jb=jb):
                last = None
                for c in range(CH):
                    sl = slice(nt * 512, (nt + 1) * 512)
                    last = v.tensor_tensor(out=abf(2, c)[:, sl], in0=abf(2, c)[:, sl],
                                           in1=jb[:, 512:1024], op=OP.subtract)
                return last
            t_prev = o("v", f2, after=[t_1], inc="V", par=True)
            t_f2[pos] = t_prev
            if pos == 0 and mid is not None:
                mid(t_prev)
        return t_prev

    t_mu = emit_stats(res_c, [TOK_LOADS, tok_cp])
    def f_xn(v, c, sl):
        v.tensor_tensor(out=JUNK, in0=res_c(c)[:, sl], in1=PB[6], op=OP.mult)
        return v.tensor_tensor(out=res_c(c)[:, sl], in0=JUNK, in1=PB[7], op=OP.subtract)
    TOK_XN = emit_norm_apply(res_c, f_xn, t_mu)

    xn_d = nc.dram_tensor("xn_d", [C, T], F32).ap()
    ydq_d = nc.dram_tensor("ydq_d", [C, T], BF16).ap()
    t_xp = None
    for c in range(CH):
        t_xp = dma("y", xn_d[c * 128:(c + 1) * 128, :], res_c(c), after=[TOK_XN])
    TOK_XNPARK = t_xp

    def qslot(j):
        return ar_f32(j // 4)[:, (j % 4) * T:(j % 4) * T + T]

    ydq_toks = []

    def emit_quant_chunk(c, after_extra):
        y, ay, z, e_, q, qi, r, m = (qslot(j) for j in range(8))
        stage = AR[0][:, 0:T]
        t_x = dma("y", y, xn_d[c * 128:(c + 1) * 128, :], after=list(after_extra))
        def f1(v, y=y, ay=ay):
            v.tensor_scalar(out=y, in0=y, scalar1=SINV, scalar2=448.0, op0=OP.mult, op1=OP.min)
            v.tensor_scalar(out=y, in0=y, scalar1=-448.0, scalar2=None, op0=OP.max)
            v.tensor_scalar(out=ay, in0=y, scalar1=-1.0, scalar2=None, op0=OP.mult)
            return v.tensor_tensor(out=ay, in0=ay, in1=y, op=OP.max)
        t1 = o("v", f1, after=[t_x, tok_scal] + list(after_extra), inc="V")
        t2 = o("s", (lambda s, ay=ay, z=z: (s.activation(out=z, in_=ay, func=AF.Square),
                                            s.activation(out=z, in_=z, func=AF.Ln))[-1]),
               after=[t1], inc="S")
        def f3(v, z=z, e_=e_):
            v.tensor_scalar(out=z, in0=z, scalar1=1.0 / (2 * LN2), scalar2=-6.0, op0=OP.mult, op1=OP.max)
            return v.tensor_scalar(out=e_, in0=z, scalar1=-5.0, scalar2=1.0, op0=OP.is_ge, op1=OP.mult)
        t_e = o("v", f3, after=[t2], inc="V")
        for k in range(-4, 9):
            def fe(en, z=z, e_=e_, k=k):
                return en.scalar_tensor_tensor(out=e_, in0=z, scalar=float(k), in1=e_,
                                               op0=OP.is_ge, op1=OP.add)
            t_e = o("v", fe, after=[t_e], inc="V")
        t4 = o("s", (lambda s, e_=e_, q=q, qi=qi:
                     (s.activation(out=q, in_=e_, func=AF.Exp, scale=cpc(LN2), bias=cpc(-9.0 * LN2)),
                      s.activation(out=qi, in_=e_, func=AF.Exp, scale=cpc(-LN2), bias=cpc(9.0 * LN2)))[-1]),
               after=[t_e], inc="S")
        def f5(v, ay=ay, qi=qi, r=r, m=m):
            v.tensor_tensor(out=r, in0=ay, in1=qi, op=OP.mult)
            return v.tensor_scalar(out=m, in0=r, scalar1=0.5, scalar2=1.0, op0=OP.is_gt, op1=OP.mult)
        t_m = o("v", f5, after=[t4], inc="V")
        for jj in range(1, 16):
            def fm(en, r=r, m=m, jj=jj):
                return en.scalar_tensor_tensor(out=m, in0=r, scalar=jj + 0.5, in1=m,
                                               op0=OP.is_gt, op1=OP.add)
            t_m = o("v", fm, after=[t_m], inc="V")
        t_sg = o("s", (lambda s, y=y, ay=ay: s.sign(out=ay, in_=y)), after=[t_m], inc="S")
        def f6(v, m=m, q=q, ay=ay, stage=stage):
            v.tensor_tensor(out=m, in0=m, in1=q, op=OP.mult)
            return v.tensor_tensor(out=stage, in0=m, in1=ay, op=OP.mult)
        t6 = o("v", f6, after=[t_sg], inc="V")
        t_out = dma("y", ydq_d[c * 128:(c + 1) * 128, :], stage, after=[t6])
        ydq_toks.append(t_out)
        return t6, t_out

    FFN_STATE = {}

    def emit_ffn(after_banks, after_ar2):
        t_yl = None
        for kc in range(CH):
            t_yl = dma("y", AR[2][:, kc * T:(kc + 1) * T], ydq_d[kc * 128:(kc + 1) * 128, :],
                       after=[after_ar2] + list(ydq_toks[-1:]))
        pieces = [load_w_piece(dram["w8"][0], cast=True),
                  load_w_piece(dram["w8"][1], cast=True)]
        ev = {0: None, 1: None}
        t_prev_mm = None
        t_st = None
        t_ac_last = None
        fx = ar_f32(4)[:, 0:512]
        fo = ar_f32(4)[:, 512:1024]
        for mc in range(CH):
            grp = mc % 2
            banks = PB[4 * grp:4 * grp + 4]
            while len(pieces) <= mc // 2:
                h = len(pieces)
                pieces.append(load_w_piece(dram["w8"][h], cast=True))
            bi, t_wp = pieces[mc // 2]
            aft = [t_yl, t_wp] + list(after_banks)
            if t_prev_mm is not None:
                aft.append(t_prev_mm)
            if ev[grp] is not None:
                aft.append(ev[grp])
            def f_mm(t, mc=mc, banks=banks, bi=bi):
                last = None
                for kc in range(CH):
                    w_ = wsl(bi, kc, (mc % 2) * 128)
                    for nt in range(NT):
                        last = t.matmul(banks[nt], w_, AR[2][:, kc * T + nt * 512:kc * T + (nt + 1) * 512],
                                        start=(kc == 0), stop=(kc == CH - 1))
                return last
            t_mm = o("t", f_mm, after=aft, inc="T")
            wb_busy[bi] = t_mm
            last_ev = None
            for nt in range(NT):
                sl = slice(nt * 512, (nt + 1) * 512)
                t_xo = dma("y", fx, x_in[mc * 128:(mc + 1) * 128, sl],
                           after=[t_mm] + ([t_ac_last] if t_ac_last else []))
                def f_ev(v, nt=nt, mc=mc, banks=banks, fo=fo):
                    return v.tensor_scalar(out=fo, in0=banks[nt], scalar1=SWS,
                                           scalar2=P["fbias"][:, mc:mc + 1],
                                           op0=OP.mult, op1=OP.add)
                t_ev = o("v", f_ev, after=[t_mm, tok_scal] + ([t_st] if t_st else []), inc="V")
                def f_acc(v, fo=fo, fx=fx):
                    return v.tensor_tensor(out=fo, in0=fo, in1=fx, op=OP.add)
                t_ac = o("v", f_acc, after=[t_ev, t_xo], inc="V")
                t_st = dma("y", acc_d[mc * 128:(mc + 1) * 128, sl], fo, after=[t_ac])
                t_ac_last = t_ac
                last_ev = t_ev
            ev[grp] = last_ev
            t_prev_mm = t_mm
        FFN_STATE["acc"] = t_st
        FFN_STATE["banks"] = max(ev[0], ev[1], key=lambda x: x[1])
        FFN_STATE["mm"] = t_prev_mm

    def emit_params(l, after_toks):
        t_ld = None
        for p in ("ln1w", "mxk", "mxv", "mxr", "td", "tf", "ln2w", "cmk", "cmr"):
            t_ld = dma("y", P[p], dram[f"{p}{l}"], after=after_toks)
        def f_v(v):
            for (a, b, mx, w_, sc) in (("ak", "bk", "mxk", "ln1w", None),
                                       ("av", "bv", "mxv", "ln1w", 0.5),
                                       ("arr", "brr", "mxr", "ln1w", 0.5),
                                       ("ack", "bck", "cmk", "ln2w", R5),
                                       ("acr", "bcr", "cmr", "ln2w", 0.5)):
                v.tensor_tensor(out=P[a], in0=P[w_], in1=P[mx], op=OP.mult)
                v.tensor_tensor(out=P[b], in0=P[w_], in1=P[a], op=OP.subtract)
                if sc is not None:
                    v.tensor_scalar(out=P[a], in0=P[a], scalar1=sc, scalar2=None, op0=OP.mult)
                    v.tensor_scalar(out=P[b], in0=P[b], scalar1=sc, scalar2=None, op0=OP.mult)
            return v.nop()
        t_pv = o("v", f_v, after=[t_ld, tok_cp], inc="V")
        t_s = o("s", (lambda s: (s.activation(out=P["wdec"], in_=P["td"], func=AF.Exp),
                                 s.activation(out=P["eu"], in_=P["tf"], func=AF.Exp))[-1]),
                after=[t_ld], inc="S")
        def f_w(v):
            v.tensor_scalar(out=P["wdec"], in0=P["wdec"], scalar1=-1.0, scalar2=None, op0=OP.mult)
            v.tensor_scalar(out=P["nw"], in0=P["wdec"], scalar1=-1.0, scalar2=None, op0=OP.mult)
            return v.tensor_scalar(out=P["b2k"], in0=P["wdec"], scalar1=2047.0, scalar2=None, op0=OP.mult)
        t_pw = o("v", f_w, after=[t_s], inc="V")
        return t_pv, t_pw

    hrecv_free = [None]
    def emit_halo_ag(l, kind, src_tok):
        i_b = ccb[f"{kind}{l}_i"]; o_b = ccb[f"{kind}{l}_o"]
        t_out = dma("y", i_b, HSEND, after=[src_tok])
        def f_cc(g, i_b=i_b, o_b=o_b):
            return g.collective_compute("AllGather", OP.bypass,
                                        replica_groups=[list(range(NCORE))],
                                        ins=[i_b.opt()], outs=[o_b.opt()])
        t_cc = o("g", f_cc, after=[t_out], inc="CC")
        t_in = dma("y", HRECV.rearrange("p (r c) -> p r c", r=8),
                   o_b.rearrange("(r p) c -> p r c", p=128),
                   after=[t_cc] + ([hrecv_free[0]] if hrecv_free[0] else []))
        def f_cmb(v):
            v.tensor_tensor(out=HRECV, in0=HRECV, in1=MASKH, op=OP.mult)
            return v.reduce_sum(out=PREVC, in_=HRECV.rearrange("p (r c) -> p c r", r=8),
                                axis=AX)
        t_c = o("v", f_cmb, after=[t_in], inc="V")
        hrecv_free[0] = t_c
        return t_c, t_out

    def emit_mixed(dst_i, acol, bcol, after_toks, after_col0=()):
        def f1(v):
            last = None
            for c in range(CH):
                last = v.tensor_scalar(out=abf(dst_i, c)[:, 1:T], in0=AR[2][:, c * T:c * T + T - 1],
                                       scalar1=bcol[:, c:c + 1], scalar2=None, op0=OP.mult)
            return last
        t1 = o("v", f1, after=after_toks, inc="V", par=True)
        def f2(v):
            last = None
            for c in range(CH):
                last = v.scalar_tensor_tensor(out=abf(dst_i, c)[:, 1:T], in0=AR[2][:, c * T + 1:c * T + T],
                                              scalar=acol[:, c:c + 1], in1=abf(dst_i, c)[:, 1:T],
                                              op0=OP.mult, op1=OP.add)
            return last
        t2 = o("v", f2, after=[t1], inc="V", par=True)
        def f3(v):
            last = None
            for c in range(CH):
                last = v.tensor_scalar(out=abf(dst_i, c)[:, 0:1], in0=PREVC[:, c:c + 1],
                                       scalar1=bcol[:, c:c + 1], scalar2=None, op0=OP.mult)
            return last
        t3 = o("v", f3, after=[t2] + list(after_col0), inc="V", par=True)
        def f4(v):
            last = None
            for c in range(CH):
                last = v.scalar_tensor_tensor(out=abf(dst_i, c)[:, 0:1], in0=AR[2][:, c * T:c * T + 1],
                                              scalar=acol[:, c:c + 1], in1=abf(dst_i, c)[:, 0:1],
                                              op0=OP.mult, op1=OP.add)
            return last
        return o("v", f4, after=[t3], inc="V", par=True)

    def emit_cc_proj(wn, rhs_i, evac_func, after_rhs, t_guard, evac_extra=()):
        pieces = [load_w_piece(dram[wn][0], cast=True),
                  load_w_piece(dram[wn][1], cast=True)]
        ev = {0: None, 1: None}
        ev_mc = {}
        t_prev_mm = t_guard
        for mc in range(CH):
            grp = mc % 2
            banks = PB[4 * grp:4 * grp + 4]
            if mc % 2 == 0 and mc // 2 + 1 < 4 and len(pieces) < mc // 2 + 2:
                pass
            while len(pieces) <= mc // 2:
                h = len(pieces)
                pieces.append(load_w_piece(dram[wn][h], cast=True))
            bi, t_wp = pieces[mc // 2]
            aft = [after_rhs, t_wp]
            if t_prev_mm is not None:
                aft.append(t_prev_mm)
            if ev[grp] is not None:
                aft.append(ev[grp])
            def f_mm(t, mc=mc, banks=banks, bi=bi):
                last = None
                for kc in range(CH):
                    w_ = wsl(bi, kc, (mc % 2) * 128)
                    for nt in range(NT):
                        last = t.matmul(banks[nt], w_, abf(rhs_i, kc)[:, nt * 512:(nt + 1) * 512],
                                        start=(kc == 0), stop=(kc == CH - 1))
                return last
            t_mm = o("t", f_mm, after=aft, inc="T")
            wb_busy[bi] = t_mm
            def f_ev(s, mc=mc, banks=banks):
                last = None
                for nt in range(NT):
                    last = evac_func(s, mc, nt, banks[nt])
                return last
            ev[grp] = o("s", f_ev, after=[t_mm] + list(evac_extra), inc="S")
            ev_mc[mc] = ev[grp]
            t_prev_mm = t_mm
        t_ev = max(ev[0], ev[1], key=lambda x: x[1])
        return t_ev, t_prev_mm, ev_mc

    def ev_copy(dst_i):
        def f(s, mc, nt, bank, dst_i=dst_i):
            return s.copy(out=abf(dst_i, mc)[:, nt * 512:(nt + 1) * 512], in_=bank)
        return f

    def ev_tanh(dst_i):
        def f(s, mc, nt, bank, dst_i=dst_i):
            return s.activation(out=abf(dst_i, mc)[:, nt * 512:(nt + 1) * 512],
                                in_=bank, func=AF.Tanh)
        return f

    trecv_free = [None]
    t_res_ok = TOK_XN
    prev_layer = TOK_XNPARK
    tmm = None
    t_hsend_free = None
    t_tots_free = None

    for l in range(L):
        t_pv, t_pw = emit_params(l, [prev_layer])
        t_mu = emit_stats(res_c, [t_res_ok] + ([TOK_XNPARK] if l == 0 else []))
        hs_box = {}
        def f_hs(v):
            last = None
            for c in range(CH):
                last = v.tensor_copy(out=HSEND[:, c:c + 1], in_=AR[2][:, c * T + T - 1:c * T + T])
            return last
        def mid_hs(tok, t_free=t_hsend_free):
            hs_box[0] = o("v", f_hs, after=[tok] + ([t_free] if t_free else []),
                          inc="V", par=True)
        t_g = emit_gfast(t_mu, first_nt=NT - 1, mid=mid_hs)
        t_hs = hs_box[0]
        t_prevc, t_hsend_free = emit_halo_ag(l, "hg", t_hs)
        t_park = None
        for c in range(CH):
            t_park = dma("y", res_d[c * 128:(c + 1) * 128, :], res_c(c), after=[t_g])
        t_mk = emit_mixed(3, P["ak"], P["bk"], [t_g, t_pv], after_col0=[t_prevc])
        t_mv = emit_mixed(4, P["av"], P["bv"], [t_mk])
        t_mr = emit_mixed(0, P["arr"], P["brr"], [t_mv, t_park])
        t_k, tmm, kev = emit_cc_proj(f"wk{l}", 3, ev_copy(1), t_mk, tmm, evac_extra=[t_park])
        t_v_, tmm, vev = emit_cc_proj(f"wv{l}", 4, ev_copy(3), t_mv, tmm, evac_extra=[t_k])

        DRV = ar_f32(4)[:, 0:T]
        t_tot = None
        for c in range(CH):
            t_dv = o("s", (lambda s, c=c:
                           s.activation(out=DRV, in_=TROW, func=AF.Exp,
                                        scale=P["nw"][:, c:c + 1], bias=P["b2k"][:, c:c + 1])),
                     after=[t_pw, tmm] + ([t_tot] if t_tot else []), inc="S")
            t_ek = o("s", (lambda s, c=c:
                           s.activation(out=hf32(0), in_=AR[1][:, c * T:(c + 1) * T],
                                        func=AF.Exp, bias=cpc(-DELTA))),
                     after=[kev[c]], inc="S")
            t_ekv = o("v", (lambda v, c=c:
                            v.tensor_tensor(out=hf32(1), in0=hf32(0), in1=abf(3, c), op=OP.mult)),
                      after=[t_ek, vev[c]], inc="V")
            def f_t1(v, c=c):
                v.scalar_tensor_tensor(out=hf32(1), in0=DRV, scalar=1.0, in1=hf32(1),
                                       op0=OP.mult, op1=OP.mult, accum_out=TOTS[:, c:c + 1])
                return v.scalar_tensor_tensor(out=hf32(0), in0=DRV, scalar=1.0, in1=hf32(0),
                                              op0=OP.mult, op1=OP.mult, accum_out=TOTS[:, 8 + c:9 + c])
            aft = [t_ekv, t_dv]
            if c == 0 and t_tots_free is not None:
                aft.append(t_tots_free)
            t_tot = o("v", f_t1, after=aft, inc="V")

        t_ti = dma("y", ccb[f"tt{l}_i"], TOTS, after=[t_tot])
        t_tots_free = t_ti
        def f_cc(g, l=l):
            return g.collective_compute("AllGather", OP.bypass,
                                        replica_groups=[list(range(NCORE))],
                                        ins=[ccb[f"tt{l}_i"].opt()], outs=[ccb[f"tt{l}_o"].opt()])
        t_cc = o("g", f_cc, after=[t_ti], inc="CC")
        t_tr = dma("y", TRECV.rearrange("p (r c) -> p r c", r=8),
                   ccb[f"tt{l}_o"].rearrange("(r p) c -> p r c", p=128),
                   after=[t_cc] + ([trecv_free[0]] if trecv_free[0] else []))
        t_rt, tmm, _rev = emit_cc_proj(f"wr{l}", 0, ev_tanh(4), t_mr, tmm,
                                       evac_extra=[t_tot])
        t_pk = dma("g", wckb_d[l], dram[f"wck{l}"])
        t_pv2 = dma("g", wcvb_d[l], dram[f"wcv{l}"])
        PREPTOK = t_pv2
        def f_w16(v):
            v.tensor_copy(out=W16[:, 0:8], in_=P["wdec"])
            v.tensor_copy(out=W16[:, 8:16], in_=P["wdec"])
            last = None
            for r_ in range(8):
                last = v.tensor_copy(out=WREP[:, r_ * 16:(r_ + 1) * 16], in_=W16)
            return last
        t_wrep = o("v", f_w16, after=[t_pw], inc="V")
        t_c1 = o("v", (lambda v: v.tensor_tensor(out=COEF, in0=WREP, in1=MTS, op=OP.mult)),
                 after=[t_wrep], inc="V")
        t_c2 = o("s", (lambda s: s.activation(out=COEF, in_=COEF, func=AF.Exp)),
                 after=[t_c1], inc="S")
        def f_c3(v, l=l):
            v.tensor_tensor(out=COEF, in0=COEF, in1=MTM, op=OP.mult)
            v.tensor_tensor(out=COEF, in0=COEF, in1=TRECV, op=OP.mult)
            return v.reduce_sum(out=ABINIT, in_=COEF.rearrange("p (r c) -> p c r", r=8),
                                axis=AX)
        t_abi = o("v", f_c3, after=[t_c2, t_tr], inc="V")
        trecv_free[0] = t_abi

        if l == 2:
            t_sc = t_abi
            for c in range(CH):
                t_eb = o("s", (lambda s, c=c:
                               s.activation(out=EWB, in_=TROW, func=AF.Exp,
                                            scale=cpc(0.0), bias=P["wdec"][:, c:c + 1])),
                         after=[t_pw, t_sc, tmm], inc="S")
                t_ek2 = o("s", (lambda s, c=c:
                                s.activation(out=hf32(0), in_=AR[1][:, c * T:(c + 1) * T],
                                             func=AF.Exp, bias=cpc(-DELTA))),
                          inc="S")
                t_ekv2 = o("v", (lambda v, c=c:
                                 v.tensor_tensor(out=hf32(1), in0=hf32(0), in1=abf(3, c), op=OP.mult)),
                           after=[t_ek2, t_abi], inc="V")
                def f_scan(v, c=c):
                    v.tensor_copy(out=ASH[:, 0:1], in_=ABINIT[:, c:c + 1])
                    v.tensor_copy(out=BSH[:, 0:1], in_=ABINIT[:, 8 + c:9 + c])
                    v.tensor_tensor_scan(out=ASH[:, 1:T + 1], data0=EWB, data1=hf32(1),
                                         initial=ABINIT[:, c:c + 1], op0=OP.mult, op1=OP.add)
                    return v.tensor_tensor_scan(out=BSH[:, 1:T + 1], data0=EWB, data1=hf32(0),
                                                initial=ABINIT[:, 8 + c:9 + c], op0=OP.mult, op1=OP.add)
                t_scan = o("v", f_scan, after=[t_ekv2, t_eb], inc="V", par=True)
                def f_nd(v, c=c):
                    v.scalar_tensor_tensor(out=hf32(1), in0=hf32(1), scalar=P["eu"][:, c:c + 1],
                                           in1=ASH[:, 0:T], op0=OP.mult, op1=OP.add)
                    return v.scalar_tensor_tensor(out=hf32(0), in0=hf32(0), scalar=P["eu"][:, c:c + 1],
                                                  in1=BSH[:, 0:T], op0=OP.mult, op1=OP.add)
                t_nd = o("v", f_nd, after=[t_scan], inc="V", par=True)
                t_ln = o("s", (lambda s: (s.activation(out=EWB, in_=hf32(0), func=AF.Ln),
                                          s.activation(out=EWB, in_=EWB, func=AF.Exp, scale=cpc(-1.0)))[-1]),
                         after=[t_nd], inc="S")
                def f_wkv(v, c=c):
                    v.tensor_tensor(out=hf32(1), in0=hf32(1), in1=EWB, op=OP.mult)
                    return v.scalar_tensor_tensor(out=abf(3, c), in0=abf(4, c), scalar=1.0,
                                                  in1=hf32(1), op0=OP.add, op1=OP.mult)
                t_sc = o("v", f_wkv, after=[t_ln, t_rt], inc="V")
            TOK_GATED = t_sc
        else:
            A0F = ar_f32(0)
            A2F = ar_f32(2)
            BUF = [
                dict(EW=A0F[:, 2 * T:3 * T], EK=hf32(0), EKV=hf32(1),
                     ASHX=A0F[:, 0:T], BSHX=A0F[:, T:2 * T]),
                dict(EW=A2F[:, 2 * T:3 * T], EK=A2F[:, 3 * T:4 * T],
                     EKV=A0F[:, 3 * T:4 * T],
                     ASHX=A2F[:, 0:T], BSHX=A2F[:, T:2 * T]),
            ]
            t_eb = {}; t_ek2 = {}; t_nd = {}; t_ln = {}; t_wkv = {}
            first_aft = [t_pw, t_abi, tmm, t_mr]

            def em_s(c):
                b = BUF[c % 2]
                aft = list(first_aft) if c < 2 else []
                t_eb[c] = o("s", (lambda s, c=c, b=b:
                                  s.activation(out=b["EW"], in_=TROW, func=AF.Exp,
                                               scale=cpc(0.0), bias=P["wdec"][:, c:c + 1])),
                            after=aft + ([t_wkv[c - 2]] if c >= 2 else []), inc="S")
                t_ek2[c] = o("s", (lambda s, c=c, b=b:
                                   s.activation(out=b["EK"], in_=AR[1][:, c * T:(c + 1) * T],
                                                func=AF.Exp, bias=cpc(-DELTA))),
                             after=([t_nd[c - 2]] if c >= 2 else []), inc="S")

            def em_v(c):
                b = BUF[c % 2]
                aft = list(first_aft) if c < 2 else []
                t_ekv2 = o("v", (lambda v, c=c, b=b:
                                 v.tensor_tensor(out=b["EKV"], in0=b["EK"], in1=abf(3, c), op=OP.mult)),
                           after=aft + [t_ek2[c]], inc="V")
                def f_scan(v, c=c, b=b):
                    v.tensor_tensor_scan(out=b["ASHX"], data0=b["EW"], data1=b["EKV"],
                                         initial=ABINIT[:, c:c + 1], op0=OP.mult, op1=OP.add)
                    return v.tensor_tensor_scan(out=b["BSHX"], data0=b["EW"], data1=b["EK"],
                                                initial=ABINIT[:, 8 + c:9 + c], op0=OP.mult, op1=OP.add)
                t_scan = o("v", f_scan, after=[t_ekv2, t_eb[c]], inc="V", par=True)
                def f_nd(v, c=c, b=b):
                    v.scalar_tensor_tensor(out=b["EKV"][:, 0:1], in0=b["EKV"][:, 0:1],
                                           scalar=P["eu"][:, c:c + 1],
                                           in1=ABINIT[:, c:c + 1], op0=OP.mult, op1=OP.add)
                    v.scalar_tensor_tensor(out=b["EKV"][:, 1:T], in0=b["EKV"][:, 1:T],
                                           scalar=P["eu"][:, c:c + 1],
                                           in1=b["ASHX"][:, 0:T - 1], op0=OP.mult, op1=OP.add)
                    v.scalar_tensor_tensor(out=b["EK"][:, 0:1], in0=b["EK"][:, 0:1],
                                           scalar=P["eu"][:, c:c + 1],
                                           in1=ABINIT[:, 8 + c:9 + c], op0=OP.mult, op1=OP.add)
                    return v.scalar_tensor_tensor(out=b["EK"][:, 1:T], in0=b["EK"][:, 1:T],
                                                  scalar=P["eu"][:, c:c + 1],
                                                  in1=b["BSHX"][:, 0:T - 1], op0=OP.mult, op1=OP.add)
                t_nd[c] = o("v", f_nd, after=[t_scan], inc="V", par=True)

            def em_ln(c):
                b = BUF[c % 2]
                t_ln[c] = o("s", (lambda s, b=b:
                                  (s.activation(out=b["EW"], in_=b["EK"], func=AF.Ln),
                                   s.activation(out=b["EW"], in_=b["EW"], func=AF.Exp,
                                                scale=cpc(-1.0)))[-1]),
                            after=[t_nd[c]], inc="S")

            def em_wkv(c):
                b = BUF[c % 2]
                def f_wkv(v, c=c, b=b):
                    v.tensor_tensor(out=b["EKV"], in0=b["EKV"], in1=b["EW"], op=OP.mult)
                    return v.scalar_tensor_tensor(out=abf(3, c), in0=abf(4, c), scalar=1.0,
                                                  in1=b["EKV"], op0=OP.add, op1=OP.mult)
                t_wkv[c] = o("v", f_wkv, after=[t_ln[c], t_rt], inc="V")

            for c in range(CH):
                em_s(c)
                if c >= 1:
                    em_ln(c - 1)
                em_v(c)
                if c >= 1:
                    em_wkv(c - 1)
            em_ln(CH - 1)
            em_wkv(CH - 1)
            TOK_GATED = t_wkv[CH - 1]
        if l == 2:
            emit_ffn([t_rt, TOK_GATED], t_mr)
            tmm = FFN_STATE["mm"]

        rl_toks = {}
        for c in range(CH):
            rl_toks[c] = dma("y", res_c(c), res_d[c * 128:(c + 1) * 128, :], after=[TOK_GATED])
        pieces = [load_w_piece(dram[f"wo{l}"][0], cast=True),
                  load_w_piece(dram[f"wo{l}"][1], cast=True)]
        ev = {0: None, 1: None}
        t_prev_mm = tmm
        for mc in range(CH):
            grp = mc % 2
            banks = PB[4 * grp:4 * grp + 4]
            while len(pieces) <= mc // 2:
                h = len(pieces)
                pieces.append(load_w_piece(dram[f"wo{l}"][h], cast=True))
            bi, t_wp = pieces[mc // 2]
            aft = [TOK_GATED, t_wp]
            if l == 2 and mc < 2:
                aft.append(FFN_STATE["banks"])
            if t_prev_mm is not None:
                aft.append(t_prev_mm)
            if ev[grp] is not None:
                aft.append(ev[grp])
            def f_mm(t, mc=mc, banks=banks, bi=bi):
                last = None
                for kc in range(CH):
                    w_ = wsl(bi, kc, (mc % 2) * 128)
                    for nt in range(NT):
                        last = t.matmul(banks[nt], w_, abf(3, kc)[:, nt * 512:(nt + 1) * 512],
                                        start=(kc == 0), stop=(kc == CH - 1))
                return last
            t_mm = o("t", f_mm, after=aft, inc="T")
            wb_busy[bi] = t_mm
            def f_ev(v, mc=mc, banks=banks):
                last = None
                for nt in range(NT):
                    sl = slice(nt * 512, (nt + 1) * 512)
                    last = v.tensor_tensor(out=res_c(mc)[:, sl], in0=res_c(mc)[:, sl],
                                           in1=banks[nt], op=OP.add)
                return last
            ev[grp] = o("v", f_ev, after=[t_mm, rl_toks[mc]], inc="V", par=True)
            t_prev_mm = t_mm
        t_res_ok = max(ev[0], ev[1], key=lambda x: x[1])
        tmm = t_prev_mm

        t_mu2 = emit_stats(res_c, [t_res_ok] +
                           ([FFN_STATE["acc"], FFN_STATE["banks"], FFN_STATE["mm"]] if l == 2 else []))
        hs2_box = {}
        def f_hs2(v):
            last = None
            for c in range(CH):
                last = v.tensor_copy(out=HSEND[:, c:c + 1], in_=AR[2][:, c * T + T - 1:c * T + T])
            return last
        def mid_hs2(tok, t_free=t_hsend_free):
            hs2_box[0] = o("v", f_hs2, after=[tok] + ([t_free] if t_free else []),
                           inc="V", par=True)
        t_g2 = emit_gfast(t_mu2, first_nt=NT - 1, mid=mid_hs2)
        t_hs2 = hs2_box[0]
        t_prevc2, t_hsend_free = emit_halo_ag(l, "h2", t_hs2)
        t_m2k = emit_mixed(3, P["ack"], P["bck"], [t_g2, t_pv], after_col0=[t_prevc2])
        t_m2r = emit_mixed(4, P["acr"], P["bcr"], [t_m2k])
        t_rt2, tmm, _r2ev = emit_cc_proj(f"wcr{l}", 4, ev_tanh(2), t_m2r, tmm, evac_extra=[t_m2r])

        qlast = [None]
        t_parkcm = None
        if l < 2:
            for c in range(CH):
                t_parkcm = dma("y", res_d[c * 128:(c + 1) * 128, :], res_c(c),
                               after=[t_mu2])
        t_cm = tmm
        t_evk_all = t_rt2
        t_kkdone = {}
        for nt in range(NT):
            t_evk = None
            evk_hist = {}
            for fg in range(16):
                bi_k, t_wk = load_w_piece(wckb_d[l][fg], cast=False, after=[PREPTOK])
                for fi in range(2):
                    f_ = fg * 2 + fi
                    bank = PB[f_ % 4]
                    aft = [t_m2k, t_wk]
                    if t_cm is not None and f_ == 0:
                        aft.append(t_cm)
                    if evk_hist.get(f_ - 4) is not None:
                        aft.append(evk_hist[f_ - 4])
                    def f_kk(t, fi=fi, bank=bank, bi_k=bi_k, nt=nt):
                        last = None
                        for kc in range(CH):
                            w_ = wsl(bi_k, kc, fi * 128)
                            last = t.matmul(bank, w_, abf(3, kc)[:, nt * 512:(nt + 1) * 512],
                                            start=(kc == 0), stop=(kc == CH - 1))
                        return last
                    t_kk = o("t", f_kk, after=aft, inc="T")
                    wb_busy[bi_k] = t_kk
                    slot = f_ % 2
                    JB = JUNK.bitcast(BF16)
                    def f_rl(s, bank=bank, slot=slot):
                        return s.activation(out=JB[:, slot * 512:(slot + 1) * 512],
                                            in_=bank, func=AF.Relu)
                    t_rl1 = o("s", f_rl, after=[t_kk] + ([t_evk_all] if (nt == 0 and fg == 0 and fi == 0) else [])
                              + ([t_evk] if t_evk else []), inc="S")
                    def f_rq(s, f_=f_, slot=slot):
                        return s.activation(out=AR[4][:, f_ * 512:(f_ + 1) * 512],
                                            in_=JB[:, slot * 512:(slot + 1) * 512],
                                            func=AF.Square)
                    t_evk = o("s", f_rq, after=[t_rl1], inc="S")
                    evk_hist[f_] = t_evk
                    t_cm = t_kk
            t_kkdone[nt] = t_evk
            t_vv = None
            for fg in range(16):
                bi_v, t_wv = load_w_piece(wcvb_d[l][fg], cast=False, after=[PREPTOK])
                aft = [t_evk, t_wv]
                if t_cm is not None:
                    aft.append(t_cm)
                def f_vv(t, fg=fg, bi_v=bi_v):
                    last = None
                    for fi in range(2):
                        f_ = fg * 2 + fi
                        for mc in range(CH):
                            w_ = WB[bi_v][:, fi * 1024 + mc * 128:fi * 1024 + mc * 128 + 128]
                            last = t.matmul(PB[mc], w_, AR[4][:, f_ * 512:(f_ + 1) * 512],
                                            start=(f_ == 0), stop=(f_ == FCH - 1))
                    return last
                t_vv = o("t", f_vv, after=aft, inc="T")
                wb_busy[bi_v] = t_vv
                t_cm = t_vv
            if l < 2:
                def f_vev(s, nt=nt):
                    last = None
                    for mc in range(CH):
                        last = s.copy(out=abf(3, mc)[:, nt * 512:(nt + 1) * 512], in_=PB[mc])
                    return last
                t_cm = o("s", f_vev, after=[t_vv, t_kkdone[nt]], inc="S", par=True)
                qc = 4 * l + nt
                tq6, tqout = emit_quant_chunk(qc, [t_parkcm, TOK_GATED])
                qlast[0] = tqout
            else:
                for mc in range(CH):
                    def f_gt(v, mc=mc, nt=nt):
                        sl = slice(nt * 512, (nt + 1) * 512)
                        v.scalar_tensor_tensor(out=JUNK,
                                               in0=AR[2][:, mc * T + nt * 512:mc * T + (nt + 1) * 512],
                                               scalar=1.0, in1=PB[mc], op0=OP.add, op1=OP.mult)
                        return v.tensor_tensor(out=res_c(mc)[:, sl], in0=res_c(mc)[:, sl],
                                               in1=JUNK[:, 0:512], op=OP.add)
                    t_cm = o("v", f_gt, after=[t_vv, t_rt2], inc="V")
            t_evk_all = t_cm
        if l < 2:
            t_rl2 = None
            for c in range(CH):
                t_rl2 = dma("y", res_c(c), res_d[c * 128:(c + 1) * 128, :],
                            after=[qlast[0], t_cm])
            def f_gv(v):
                last = None
                for mc in range(CH):
                    last = v.scalar_tensor_tensor(out=AR[3][:, mc * T:(mc + 1) * T],
                                                  in0=AR[2][:, mc * T:(mc + 1) * T],
                                                  scalar=1.0, in1=AR[3][:, mc * T:(mc + 1) * T],
                                                  op0=OP.add, op1=OP.mult)
                return last
            t_gv = o("v", f_gv, after=[t_cm, t_rt2], inc="V", par=True)
            def f_ra(v):
                last = None
                for mc in range(CH):
                    last = v.tensor_tensor(out=res_c(mc), in0=res_c(mc),
                                           in1=AR[3][:, mc * T:(mc + 1) * T], op=OP.add)
                return last
            t_res_ok = o("v", f_ra, after=[t_gv, t_rl2], inc="V", par=True)
        else:
            t_res_ok = t_cm
        prev_layer = t_res_ok

    t_muf = emit_stats(res_c, [t_res_ok])
    def accslot(c):
        return ar_f32(3 + c // 4)[:, (c % 4) * T:(c % 4) * T + T]
    t_al = None
    for c in range(CH):
        t_al = dma("y", accslot(c), acc_d[c * 128:(c + 1) * 128, :],
                   after=[FFN_STATE["acc"], t_res_ok])
    HF = HALF.bitcast(F32)
    t_f2 = {}
    t_out_last = None
    for nt in range(NT):
        sl = slice(nt * 512, (nt + 1) * 512)
        pair = (6, 7) if nt % 2 == 0 else (4, 5)
        t_b = emit_bcast(nt, [t_muf] + ([t_f2[nt - 2]] if (nt - 2) in t_f2 else []), pair=pair)
        def f1(v, sl=sl, pair=pair):
            last = None
            for c in range(CH):
                last = v.tensor_tensor(out=HF[:, c * 512:c * 512 + 512],
                                       in0=res_c(c)[:, sl], in1=PB[pair[0]], op=OP.mult)
            return last
        t_1 = o("v", f1, after=[t_b, t_al], inc="V", par=True)
        def f2(v, sl=sl, pair=pair):
            last = None
            for c in range(CH):
                last = v.tensor_tensor(out=HF[:, c * 512:c * 512 + 512],
                                       in0=HF[:, c * 512:c * 512 + 512],
                                       in1=PB[pair[1]], op=OP.subtract)
            return last
        t_2 = o("v", f2, after=[t_1], inc="V", par=True)
        def f3(v, sl=sl):
            last = None
            for c in range(CH):
                last = v.scalar_tensor_tensor(out=accslot(c)[:, sl],
                                              in0=HF[:, c * 512:c * 512 + 512],
                                              scalar=P["lnfw"][:, c:c + 1],
                                              in1=accslot(c)[:, sl],
                                              op0=OP.mult, op1=OP.add)
            return last
        t_3 = o("v", f3, after=[t_2], inc="V", par=True)
        t_f2[nt] = t_3
    for c in range(CH):
        t_out_last = dma("y", out_d[c * 128:(c + 1) * 128, :], accslot(c),
                         after=[t_f2[NT - 1]])
    o("y", lambda e: None, after=[t_out_last])

    stack = contextlib.ExitStack()
    sems = {}
    for n in ("T", "V", "S", "G", "GD", "Y", "CC", "VX", "SX", "GX"):
        sems[n] = stack.enter_context(nc.semaphore("sem" + n))

    COMPUTE_OPS = {
        "tensor_tensor", "tensor_add", "tensor_sub", "tensor_mul", "tensor_max",
        "tensor_scalar", "tensor_single_scalar", "tensor_scalar_add",
        "tensor_scalar_sub", "tensor_scalar_mul", "tensor_scalar_max",
        "tensor_scalar_min", "tensor_relu", "tensor_reduce", "reduce_max",
        "reduce_sum", "scalar_tensor_tensor", "tensor_tensor_scan",
        "tensor_copy", "memset", "memzero", "reciprocal", "select", "transpose",
        "activation", "sign", "square", "sqrt", "mul", "copy", "add",
    }

    class SerialEng:

        def __init__(self, eng, xsem):
            self._eng = eng
            self._x = xsem
            self._xcnt = [0]
            self._pending = [None]
            self._last_sig = [None]
            self._par = [False]
            self._par_first = [False]

        def _seal_pending_x(self):
            if self._pending[0] is not None:
                self._xcnt[0] += 1
                self._pending[0].then_inc(self._x, 1)
                self._last_sig[0] = (self._x, self._xcnt[0])
                self._pending[0] = None

        def seal_op(self, sem, amt, val):
            if self._pending[0] is None:
                self._pending[0] = self._eng.nop()
            self._pending[0].then_inc(sem, amt)
            self._pending[0] = None
            self._last_sig[0] = (sem, val)

        def __getattr__(self, name):
            real = getattr(self._eng, name)
            if name not in COMPUTE_OPS:
                return real

            def call(*a, **kw):
                self._seal_pending_x()
                if self._last_sig[0] is not None and (not self._par[0] or self._par_first[0]):
                    s_, v_ = self._last_sig[0]
                    self._eng.wait_ge(s_, v_)
                self._par_first[0] = False
                inst = real(*a, **kw)
                self._pending[0] = inst
                return inst
            return call

        def begin_par(self):
            self._par[0] = True
            self._par_first[0] = True

        def end_par(self):
            self._par[0] = False

    with nc.Block() as block:
        def emitter(engname):
            serial = engname in ("v", "s", "g")
            xsem = {"v": "VX", "s": "SX", "g": "GX"}.get(engname)

            def run(eng):
                we = SerialEng(eng, sems[xsem]) if serial else eng
                for fn, waits, tok, amt, par in sched.stream[engname]:
                    for s_, v_ in waits:
                        eng.wait_ge(sems[s_], v_)
                    if par and serial:
                        we.begin_par()
                    inst = fn(we)
                    if par and serial:
                        we.end_par()
                    if tok is not None:
                        if tok[0] in ("GD", "Y", "CC"):
                            inst.then_inc(sems[tok[0]], amt)
                        elif serial:
                            we.seal_op(sems[tok[0]], amt, tok[1])
                        else:
                            if inst is None:
                                inst = eng.nop()
                            inst.then_inc(sems[tok[0]], amt)
                if serial:
                    we._seal_pending_x()
            return run
        block.tensor(emitter("t"))
        block.vector(emitter("v"))
        block.scalar(emitter("s"))
        block.gpsimd(emitter("g"))
        block.sync(emitter("y"))
    stack.close()
    return nc


_ncache = {}


def kernel(**inputs):
    x = np.asarray(inputs["L0_attn_x"], np.float32).reshape(16384, 1024)
    xT = np.ascontiguousarray(x.T)

    def colpack(v):
        return np.ascontiguousarray(np.asarray(v, np.float32).reshape(CH, 128).T)

    def piece_pack(wT, pw):
        K, O = wT.shape
        kch, npc = K // 128, O // pw
        a = wT.reshape(kch, 128, npc, pw)
        return np.ascontiguousarray(a.transpose(2, 1, 0, 3).reshape(npc, 128, kch * pw))

    def fpack(wT):
        a = wT.reshape(16, 2, 128, 1024)
        return np.ascontiguousarray(a.transpose(0, 2, 1, 3).reshape(16, 128, 2048))

    common = {}
    for l in range(L):
        common[f"wk{l}"] = piece_pack(np.asarray(inputs["L0_attn_tm_k_w"][l], np.float32).T, 256)
        common[f"wv{l}"] = piece_pack(np.asarray(inputs["L0_attn_tm_v_w"][l], np.float32).T, 256)
        common[f"wr{l}"] = piece_pack(np.asarray(inputs["L0_attn_tm_r_w"][l], np.float32).T, 256)
        common[f"wo{l}"] = piece_pack(np.asarray(inputs["L0_attn_tm_o_w"][l], np.float32).T, 256)
        common[f"wck{l}"] = piece_pack(np.asarray(inputs["L0_attn_cm_k_w"][l], np.float32).T, 256)
        common[f"wcv{l}"] = fpack(np.asarray(inputs["L0_attn_cm_v_w"][l], np.float32).T)
        common[f"wcr{l}"] = piece_pack(np.asarray(inputs["L0_attn_cm_r_w"][l], np.float32).T, 256)
        common[f"ln1w{l}"] = colpack(inputs["L0_attn_ln1_w"][l])
        common[f"mxk{l}"] = colpack(np.asarray(inputs["L0_attn_tm_mix_k"])[l].reshape(-1))
        common[f"mxv{l}"] = colpack(np.asarray(inputs["L0_attn_tm_mix_v"])[l].reshape(-1))
        common[f"mxr{l}"] = colpack(np.asarray(inputs["L0_attn_tm_mix_r"])[l].reshape(-1))
        common[f"td{l}"] = colpack(inputs["L0_attn_time_decay"][l])
        common[f"tf{l}"] = colpack(inputs["L0_attn_time_first"][l])
        common[f"ln2w{l}"] = colpack(inputs["L0_attn_ln2_w"][l])
        common[f"cmk{l}"] = colpack(np.asarray(inputs["L0_attn_cm_mix_k"])[l].reshape(-1))
        common[f"cmr{l}"] = colpack(np.asarray(inputs["L0_attn_cm_mix_r"])[l].reshape(-1))
    common["w8"] = piece_pack(np.asarray(inputs["L0_ffn_weight_fp8"], np.float32).T, 256)
    common["lnfw"] = colpack(inputs["L0_attn_lnf_w"])
    common["fbias"] = colpack(inputs["L0_ffn_bias"])
    common["scol"] = np.ascontiguousarray(
        np.broadcast_to(np.asarray(inputs["L0_ffn_input_scale"], np.float32).reshape(1, 1), (128, 1)))
    common["wscol"] = np.ascontiguousarray(
        np.broadcast_to(np.asarray(inputs["L0_ffn_weight_scale"], np.float32).reshape(1, 1), (128, 1)))

    statlh = np.zeros((128, 16), BF)
    for nt in range(NT):
        statlh[:, 4 * nt + nt] = BF(1.0)
    common["statlh"] = statlh
    eyeb = np.zeros((4, 512), np.float32)
    for nt in range(NT):
        eyeb[nt, 128 * nt:128 * (nt + 1)] = 1.0
    common["eyeb"] = eyeb
    common["trow"] = np.ascontiguousarray(
        np.broadcast_to(np.arange(T, dtype=np.float32)[None, :], (128, T)))

    in_maps = []
    for i in range(NCORE):
        m = dict(common)
        m["x"] = np.ascontiguousarray(xT[:, i * T:(i + 1) * T])
        mh = np.zeros((128, 8, 8), np.float32)
        if i > 0:
            mh[:, i - 1, :] = 1.0
        m["maskh"] = np.ascontiguousarray(mh.reshape(128, 64))
        steps = np.zeros((128, 8, 16), np.float32)
        mask = np.zeros((128, 8, 16), np.float32)
        for r in range(i):
            steps[:, r, :] = 2048.0 * (i - 1 - r)
            mask[:, r, :] = 1.0
        m["mtot_steps"] = np.ascontiguousarray(steps.reshape(128, 128))
        m["mtot_mask"] = np.ascontiguousarray(mask.reshape(128, 128))
        in_maps.append(m)

    if "nc" not in _ncache:
        _ncache["nc"] = build_program()
    nc = _ncache["nc"]
    res = run_bass_kernel_spmd(nc, in_maps, core_ids=list(range(NCORE)))
    outT = np.concatenate([np.asarray(res.results[i]["o"], np.float32) for i in range(NCORE)], axis=1)
    return np.ascontiguousarray(outT.T).reshape(1, 16384, 1024)
